# version 40
# baseline (speedup 1.0000x reference)
import numpy as np
import concourse.bass as bass
import concourse.mybir as mybir
from concourse.bass_utils import run_bass_kernel_spmd

F32 = mybir.dt.float32
I32 = mybir.dt.int32
BF16 = mybir.dt.bfloat16
AX = mybir.AxisListType.X
OP = mybir.AluOpType
ACT = mybir.ActivationFunctionType

D = 2048
NDC = 16
S = 4096
SL = 1024
H = 16
HD = 128
G = 2
KVF = 256
NT_F = 32
NT_L = 8
NGRP = 8
EPS = 1e-5
QEPS = 1e-5
TWO_PI = float(2.0 * np.pi)
PI_F = float(np.float32(np.pi))
PI_LO = float(np.nextafter(np.float32(np.pi), np.float32(0.0)))
INV_2PI = float(np.float32(1.0 / (2.0 * np.pi)))
LN1E4 = float(np.log(10000.0))

B_XLNT_LOC = 0
B_ATTNT = 0
B_XN2T = 0
B_XLNT_GRP = 16384
B_WCOL = 16384
B_WQT = 16384
B_WQROW = 20480
B_KT = 24576
B_FWB = 24576
B_VTOK = 32768
B_QSCR = 32768
B_QT = 40960
B_Y = 40960
B_XLN = 57344
B_VTSCR = 59392
B_PROBS = 60416
B_SQSCR = 60416
B_WK = 62464
B_WV = 66560
B_BETBF = 70656
B_ONES = 70672
B_ONESBF = B_WK + 2048
B_IDENT = 70688
NB = 70848

F_X1 = 0
F_XB = 0
F_QCOS = 4096
F_QSIN = 5120
F_KCOS = 6144
F_KSIN = 7168
F_TG1 = 8192
F_TG2 = 9216
F_TGI = 10240
F_RTMP = 11264
F_RU = 12288
F_ROWSTG = 13312
F_ACCV = 12288
F_ACCG = 13312
F_RECIPB = 14336
F_ONESROW = 15360
F_POSROW = 15488
F_THROW = 15490
F_OSB = 16384
F_RS = 17408
F_NEGM = 17412
F_SSQ = 17416
F_VARP = 17420
F_ST = 17424
F_RSTD = 17428
F_BQ = 17440
F_BQSH = 17456
F_BO = 17472
F_FB = 17488
F_GAM = 17504
F_CB = 17520
F_BK = 17536
F_BKSH = 17538
F_BV = 17540
F_WABS = 17544
F_WABS1 = 17560
F_THR = 17561
F_NTHR = 17562
F_SVQ = 17563
F_SVK = 17564
F_POSB = 17565
F_INVF = 17566
F_ONESP = 17567
F_SPACER = 17568
F_CEPS = 17569
F_IDENTF = 17600
NF = 17728


class Sem:
    def __init__(self, h):
        self.h = h
        self.n = 0

    def inc(self, k=1):
        self.n += k
        return self.n


def build_nc():
    nc = bass.Bass("TRN2", target_bir_lowering=False)

    x_full = nc.dram_tensor("x_full", [S, D], F32, kind="ExternalInput")
    x_loc = nc.dram_tensor("x_loc", [SL, D], F32, kind="ExternalInput")
    pos_base = nc.dram_tensor("pos_base", [1, 1], F32, kind="ExternalInput")
    wq = nc.dram_tensor("wq", [D, D], F32, kind="ExternalInput")
    wk = nc.dram_tensor("wk", [D, KVF], F32, kind="ExternalInput")
    wv = nc.dram_tensor("wv", [D, KVF], F32, kind="ExternalInput")
    wo = nc.dram_tensor("wo", [D, D], F32, kind="ExternalInput")
    bq = nc.dram_tensor("bq", [D], F32, kind="ExternalInput")
    bk = nc.dram_tensor("bk", [KVF], F32, kind="ExternalInput")
    bv = nc.dram_tensor("bv", [KVF], F32, kind="ExternalInput")
    bo = nc.dram_tensor("bo", [D], F32, kind="ExternalInput")
    fw = nc.dram_tensor("fw", [D, D], F32, kind="ExternalInput")
    fb = nc.dram_tensor("fb", [D], F32, kind="ExternalInput")
    gam = nc.dram_tensor("gam", [D], F32, kind="ExternalInput")
    bet = nc.dram_tensor("bet", [D], F32, kind="ExternalInput")
    out = nc.dram_tensor("out", [SL, D], F32, kind="ExternalOutput")
    dbg = nc.dram_tensor("dbg", [4096, 2048], F32, kind="ExternalOutput") if _DBG else None

    ab = nc.alloc_sbuf_tensor("ab", [128, NB], BF16)
    af = nc.alloc_sbuf_tensor("af", [128, NF], F32)

    def A(off, w, p0=0, pn=128):
        return bass.AP(ab, p0 * NB + off, [[NB, pn], [1, w]])

    def A3(off, step, cnt, w, p0=0, pn=128):
        return bass.AP(ab, p0 * NB + off, [[NB, pn], [step, cnt], [1, w]])

    def F(off, w, p0=0, pn=128):
        return bass.AP(af, p0 * NF + off, [[NF, pn], [1, w]])

    def F3(off, step, cnt, w, p0=0, pn=128):
        return bass.AP(af, p0 * NF + off, [[NF, pn], [step, cnt], [1, w]])

    def FI(off, w):
        return bass.AP(af, off, [[NF, 128], [1, w]]).bitcast(I32)

    ops = {k: [] for k in ("tensor", "vector", "scalar", "gpsimd", "sync")}

    def em(eng, fn):
        ops[eng].append(fn)

    def wait(eng, sem, val):
        if val > 0:
            em(eng, lambda e, _s=sem.h, _v=val: e.wait_ge(_s, _v))

    with (
        nc.psum_tensor("pA", [128, 1024], F32) as pA,
        nc.psum_tensor("pB", [128, 1024], F32) as pB,
        nc.psum_tensor("pC", [128, 1024], F32) as pC,
        nc.psum_tensor("pD", [128, 1024], F32) as pD,
        nc.semaphore("sdx") as _sdx,
        nc.semaphore("sdw") as _sdw,
        nc.semaphore("sg") as _sg,
        nc.semaphore("sv") as _sv,
        nc.semaphore("sc") as _sc,
        nc.semaphore("st") as _st,
        nc.Block() as block,
    ):
        sdx, sdw, sg, sv, sc, st = (Sem(x) for x in (_sdx, _sdw, _sg, _sv, _sc, _st))
        sdt = [Sem(nc.alloc_semaphore(name=f"sdt{i}")) for i in range(4)]
        sdxl, sdo, sdp, sdwk, sdbr = (Sem(nc.alloc_semaphore(name=n)) for n in
                                      ("sdxl", "sdo", "sdp", "sdwk", "sdbr"))
        sdf = [Sem(nc.alloc_semaphore(name=f"sdf{i}")) for i in range(4)]
        sdq = [Sem(nc.alloc_semaphore(name=f"sdq{i}")) for i in range(2)]
        sdqa = [Sem(nc.alloc_semaphore(name=f"sdqa{i}")) for i in range(4)]
        RECIP_ROW = lambda w=1024, o=0: bass.AP(ab, B_XLN + 2 * o, [[NB, 1], [1, 2 * w]]).bitcast(F32)

        B_PRPAIR = B_VTSCR
        B_PRPAIR2 = B_WQROW + 2048

        def PROBS_OFF(k):
            b = k % 6
            if b < 2:
                return B_PROBS + b * 1024
            if b < 4:
                return B_WQROW + (b - 2) * 1024
            return B_WK + (b - 4) * 1024
        psum_half = {
            ("pA", 0): pA[:, 0:512], ("pA", 1): pA[:, 512:1024],
            ("pB", 0): pB[:, 0:512], ("pB", 1): pB[:, 512:1024],
            ("pC", 0): pC[:, 0:512], ("pC", 1): pC[:, 512:1024],
        }

        em("gpsimd", lambda e: e.iota(F(F_TG1, 128), [[0, 128]], channel_multiplier=1,
                                      allow_small_or_imprecise_dtypes=True))
        em("gpsimd", lambda e: e.iota(F(F_TG1 + 128, 128), [[1, 128]], channel_multiplier=0,
                                      allow_small_or_imprecise_dtypes=True))
        em("gpsimd", lambda e: e.iota(F(F_INVF, 1, 0, 64), [[0, 1]], channel_multiplier=1,
                                      allow_small_or_imprecise_dtypes=True).then_inc(sg.h, 1))
        g_iv = sg.inc()
        wait("gpsimd", sg, g_iv)
        em("gpsimd", lambda e: e.tensor_copy(F(F_INVF, 1, 64, 64), F(F_INVF, 1, 0, 64)))
        em("gpsimd", lambda e: e.iota(F(F_TG2, 1024), [[1, 1024]], channel_multiplier=0,
                                      allow_small_or_imprecise_dtypes=True).then_inc(sg.h, 1))
        g_setup = sg.inc()

        em("gpsimd", lambda e: e.dma_start(out=F(F_POSROW, 1, 0, 1),
                                           in_=pos_base.ap()[:, :]).then_inc(sdp.h, 16))
        w_pos = sdp.inc(16)
        for half in range(2):
            em("gpsimd", lambda e, _h=half: e.dma_start(
                out=A3(B_WK + _h * 8 * 256, 256, 8, 256),
                in_=bass.AP(wk, _h * 8 * 128 * KVF, [[KVF, 128], [128 * KVF, 8], [1, 256]]))
                .then_inc(sdwk.h, 16))
            sdwk.inc(16)
            em("gpsimd", lambda e, _h=half: e.dma_start(
                out=A3(B_WV + _h * 8 * 256, 256, 8, 256),
                in_=bass.AP(wv, _h * 8 * 128 * KVF, [[KVF, 128], [128 * KVF, 8], [1, 256]]))
                .then_inc(sdwk.h, 16))
            sdwk.inc(16)
        em("gpsimd", lambda e: e.dma_start(out=F(F_BK, 2), in_=bass.AP(bk, 0, [[1, 128], [128, 2]]),
                                           allow_slow_non_contiguous=True).then_inc(sdwk.h, 16))
        sdwk.inc(16)
        em("gpsimd", lambda e: e.dma_start(out=F(F_BV, 2), in_=bass.AP(bv, 0, [[1, 128], [128, 2]]),
                                           allow_slow_non_contiguous=True).then_inc(sdwk.h, 16))
        w_small = sdwk.inc(16)

        wait("vector", sg, g_setup)
        em("vector", lambda e: e.tensor_tensor(F(F_IDENTF, 128), F(F_TG1, 128), F(F_TG1 + 128, 128),
                                               OP.is_equal).then_inc(sv.h, 1))
        _m = sv.inc()
        wait("vector", sv, _m)
        em("vector", lambda e: e.tensor_copy(A(B_IDENT, 128), F(F_IDENTF, 128)))
        em("vector", lambda e: e.memset(A(B_ONES, 1), 1.0))
        em("vector", lambda e: e.memset(F(F_ONESP, 1), 1.0))
        em("vector", lambda e: e.memset(F(F_ONESROW, 128), 1.0))
        em("vector", lambda e: e.memset(F(F_SVQ, 1, 0, 64), -1.0 / 128.0))
        em("vector", lambda e: e.memset(F(F_SVQ, 1, 64, 64), 1.0 / 128.0))
        em("vector", lambda e: e.memset(F(F_CEPS, 1), EPS))
        em("vector", lambda e: e.memset(F(F_SVK, 1, 0, 64), -1.0))
        em("vector", lambda e: e.memset(F(F_SVK, 1, 64, 64), 1.0))
        em("vector", lambda e: e.tensor_scalar(F(F_INVF, 1), F(F_INVF, 1), -LN1E4 / 64.0, None,
                                               OP.mult).then_inc(sv.h, 1))
        v_setup = sv.inc()
        wait("scalar", sv, v_setup)
        em("scalar", lambda e: e.activation(F(F_INVF, 1), F(F_INVF, 1), ACT.Exp).then_inc(sc.h, 1))
        c_invf = sc.inc()

        wait("tensor", sdp, w_pos)
        wait("tensor", sv, v_setup)
        em("tensor", lambda e: e.matmul(pC[:, 0:1], F(F_ONESROW, 128, 0, 1), F(F_POSROW, 1, 0, 1),
                                        start=True, stop=True).then_inc(st.h, 1))
        t_posb = st.inc()
        wait("vector", st, t_posb)
        em("vector", lambda e: e.tensor_copy(F(F_POSB, 1), pC[:, 0:1]).then_inc(sv.h, 1))
        v_posb = sv.inc()

        bias_rows = [(bq, F_BQ, "f"), (bo, F_BO, "f"), (fb, F_FB, "f"), (gam, F_GAM, "f"),
                     (bet, B_BETBF, "b")]
        prev_consume = 0
        bias_done_marks = []
        for i, (t, dst, dt) in enumerate(bias_rows):
            wait("gpsimd", st, prev_consume)
            em("gpsimd", lambda e, _t=t: e.dma_start(out=F(F_ROWSTG, 2048, 0, 1),
                                                     in_=bass.AP(_t, 0, [[2048, 1], [1, 2048]]))
               .then_inc(sdbr.h, 16))
            dmark = sdbr.inc(16)
            wait("tensor", sdbr, dmark)
            if bias_done_marks:
                wait("tensor", sv, bias_done_marks[-1])
            for c in range(NDC):
                last = c == NDC - 1
                def mk(e, _c=c, _last=last):
                    ins = e.matmul(pC[:, 16 + _c:17 + _c], F(F_ROWSTG + _c * 128, 128, 0, 1),
                                   F(F_ONESP, 1, 0, 1), start=True, stop=True)
                    if _last:
                        ins.then_inc(st.h, 1)
                em("tensor", mk)
            prev_consume = st.inc()
            wait("vector", st, prev_consume)
            if dt == "f":
                em("vector", lambda e, _d=dst: e.tensor_copy(F(_d, 16), pC[:, 16:32]).then_inc(sv.h, 1))
            else:
                em("vector", lambda e, _d=dst: e.tensor_copy(A(_d, 16), pC[:, 16:32]).then_inc(sv.h, 1))
            bias_done_marks.append(sv.inc())
        t_bias_done = prev_consume

        wait("vector", sdwk, w_small)
        wait("vector", sv, bias_done_marks[0])
        em("vector", lambda e: e.tensor_copy(F(F_BQSH, 16, 0, 64), F(F_BQ, 16, 64, 64)))
        em("vector", lambda e: e.tensor_copy(F(F_BQSH, 16, 64, 64), F(F_BQ, 16, 0, 64)))
        em("vector", lambda e: e.tensor_copy(F(F_BKSH, 2, 0, 64), F(F_BK, 2, 64, 64)))
        em("vector", lambda e: e.tensor_copy(F(F_BKSH, 2, 64, 64), F(F_BK, 2, 0, 64)).then_inc(sv.h, 1))
        v_bsh = sv.inc()

        def vstep(fn):
            em("vector", lambda e, _f=fn: _f(e).then_inc(sv.h, 1))
            m = sv.inc()
            wait("vector", sv, m)
            return m

        def range_reduce(dst_off, src_off, width):
            vstep(lambda e: e.tensor_scalar(F(F_RTMP, width), F(src_off, width), INV_2PI,
                                            None, OP.mult))
            vstep(lambda e: e.tensor_copy(FI(F_TGI, width), F(F_RTMP, width)))
            vstep(lambda e: e.tensor_copy(F(F_RTMP, width), FI(F_TGI, width)))
            vstep(lambda e: e.scalar_tensor_tensor(F(dst_off, width), F(F_RTMP, width),
                                                   -TWO_PI, F(src_off, width), OP.mult, OP.add))
            vstep(lambda e: e.tensor_scalar(F(F_RTMP, width), F(dst_off, width), PI_F,
                                            None, OP.is_gt))
            vstep(lambda e: e.scalar_tensor_tensor(F(dst_off, width), F(F_RTMP, width),
                                                   -TWO_PI, F(dst_off, width), OP.mult, OP.add))
            vstep(lambda e: e.tensor_scalar(F(dst_off, width), F(dst_off, width), PI_LO,
                                            -PI_LO, OP.min, OP.max))

        wait("vector", sv, v_posb)
        wait("vector", sc, c_invf)
        vstep(lambda e: e.tensor_scalar(F(F_TG2, 1024), F(F_TG2, 1024), F(F_POSB, 1),
                                        None, OP.add))
        vstep(lambda e: e.tensor_scalar(F(F_TG1, 1024), F(F_TG2, 1024), F(F_INVF, 1),
                                        None, OP.mult))
        range_reduce(F_QSIN, F_TG1, 1024)
        vstep(lambda e: e.tensor_scalar(F(F_TG2, 1024), F(F_TG1, 1024), PI_F / 2.0,
                                        None, OP.add))
        range_reduce(F_QCOS, F_TG2, 1024)
        em("vector", lambda e: e.tensor_copy(F(F_SPACER, 1), F(F_ONESP, 1)).then_inc(sv.h, 1))
        v_qred = sv.inc()
        wait("scalar", sv, v_qred)
        em("scalar", lambda e: e.activation(F(F_QSIN, 1024), F(F_QSIN, 1024), ACT.Sin))
        em("scalar", lambda e: e.activation(F(F_QCOS, 1024), F(F_QCOS, 1024), ACT.Sin)
           .then_inc(sc.h, 1))
        c_qtab = sc.inc()
        wait("vector", sc, c_qtab)
        em("vector", lambda e: e.tensor_scalar(F(F_QSIN, 1024), F(F_QSIN, 1024), F(F_SVQ, 1),
                                               None, OP.mult))
        em("vector", lambda e: e.tensor_scalar(F(F_QCOS, 1024), F(F_QCOS, 1024), 1.0 / 128.0,
                                               None, OP.mult).then_inc(sv.h, 1))
        v_qtab = sv.inc()

        ktab_seg_marks = []
        prev_segsin = 0
        for seg in range(4):
            if prev_segsin:
                wait("gpsimd", sc, prev_segsin)
            else:
                wait("gpsimd", sv, v_qtab)
            em("gpsimd", lambda e, _s=seg: e.iota(F(F_TG2, 1024), [[1, 1024]], base=_s * 1024,
                                                  channel_multiplier=0,
                                                  allow_small_or_imprecise_dtypes=True)
               .then_inc(sg.h, 1))
            smark = sg.inc()
            wait("vector", sg, smark)
            vstep(lambda e: e.tensor_scalar(F(F_TG1, 1024), F(F_TG2, 1024), F(F_INVF, 1),
                                            None, OP.mult))
            range_reduce(F_TG2, F_TG1, 1024)
            vstep(lambda e: e.tensor_scalar(F(F_TG1, 1024), F(F_TG1, 1024), PI_F / 2.0,
                                            None, OP.add))
            range_reduce(F_RU, F_TG1, 1024)
            em("vector", lambda e: e.tensor_copy(F(F_SPACER, 1), F(F_ONESP, 1)).then_inc(sv.h, 1))
            segred = sv.inc()
            wait("scalar", sv, segred)
            em("scalar", lambda e, _s=seg: e.activation(A(B_KT + 4096 + _s * 1024, 1024),
                                                        F(F_TG2, 1024), ACT.Sin))
            em("scalar", lambda e, _s=seg: e.activation(A(B_KT + _s * 1024, 1024),
                                                        F(F_RU, 1024), ACT.Sin).then_inc(sc.h, 1))
            segsin = sc.inc()
            prev_segsin = segsin
            wait("vector", sc, segsin)
            em("vector", lambda e, _s=seg: e.tensor_scalar(
                A(B_KT + 4096 + _s * 1024, 1024), A(B_KT + 4096 + _s * 1024, 1024),
                F(F_SVK, 1), None, OP.mult).then_inc(sv.h, 1))
            ktab_seg_marks.append(sv.inc())
        v_ktabs = sv.n

        apply_marks = {}

        LASTSQ = [0]

        def ln_tile(key, src_fn, jr, out_fn, apply_gate=0, src_gate=0):
            em("vector", lambda e: e.memset(F(F_SSQ + jr, 1), 0.0))
            if src_gate:
                wait("vector", sv, src_gate)
            em("vector", lambda e, f=src_fn: e.reduce_sum(F(F_RS + jr, 1), f(), axis=AX)
               .then_inc(sv.h, 1))
            m1 = sv.inc()
            wait("vector", sv, m1)
            em("vector", lambda e: e.tensor_scalar(F(F_NEGM + jr, 1), F(F_RS + jr, 1), -1.0 / D,
                                                   None, OP.mult).then_inc(sv.h, 1))
            m2 = sv.inc()
            wait("scalar", sv, m2)
            wait("scalar", sc, LASTSQ[0])
            em("scalar", lambda e, f=src_fn: e.activation(A(B_SQSCR, 2048), f(), ACT.Square,
                                                          bias=F(F_NEGM + jr, 1),
                                                          accum_out=F(F_SSQ + jr, 1)).then_inc(sc.h, 1))
            m3 = sc.inc()
            LASTSQ[0] = m3
            wait("vector", sc, m3)
            em("vector", lambda e: e.tensor_scalar(F(F_VARP + jr, 1), F(F_SSQ + jr, 1), 1.0 / D,
                                                   EPS, OP.mult, OP.add).then_inc(sv.h, 1))
            m4 = sv.inc()
            wait("scalar", sv, m4)
            em("scalar", lambda e: e.activation(F(F_ST + jr, 1), F(F_VARP + jr, 1), ACT.Sqrt)
               .then_inc(sc.h, 1))
            m5 = sc.inc()
            wait("vector", sc, m5)
            em("vector", lambda e: e.reciprocal(F(F_RSTD + jr, 1), F(F_ST + jr, 1)).then_inc(sv.h, 1))
            m6 = sv.inc()
            wait("vector", sv, m6)
            if apply_gate:
                wait("vector", st, apply_gate)
            em("vector", lambda e, f=src_fn, o=out_fn: e.tensor_scalar(
                o(), f(), F(F_NEGM + jr, 1), F(F_RSTD + jr, 1), OP.add, OP.mult)
                .then_inc(sv.h, 1))
            apply_marks[key] = sv.inc()

        tp_gate = []

        def transpose16(xln_off, dest_fn):
            for r in range(4):
                half = r % 2
                if len(tp_gate) >= 2:
                    wait("tensor", sv, tp_gate[-2])
                for cth in range(4):
                    dc = 4 * r + cth
                    last = cth == 3
                    def mk(e, _dc=dc, _half=half, _cth=cth, _xo=xln_off, _last=last):
                        dstv = pA[:, _half * 512:(_half + 1) * 512].bitcast(BF16)
                        ins = e.transpose(dstv[:, _cth * 128:(_cth + 1) * 128],
                                          A(_xo + _dc * 128, 128), A(B_IDENT, 128))
                        if _last:
                            ins.then_inc(st.h, 1)
                    em("tensor", mk)
                tmark = st.inc()
                wait("vector", st, tmark)
                em("vector", lambda e, _half=half, _r=r, _dfn=dest_fn: e.tensor_copy(
                    _dfn(_r), pA[:, _half * 512:(_half + 1) * 512].bitcast(BF16)[:, 0:512])
                    .then_inc(sv.h, 1))
                tp_gate.append(sv.inc())

        tp_gate2 = []

        def p_stats(jr, src_fn, dmark):
            dsem, dval = dmark
            em("vector", lambda e: e.memset(F(F_SSQ + jr, 1), 0.0).then_inc(sv.h, 1))
            zmark = sv.inc()
            wait("vector", dsem, dval)
            em("vector", lambda e, f=src_fn: e.reduce_sum(F(F_RS + jr, 1), f(), axis=AX)
               .then_inc(sv.h, 1))
            rmark = sv.inc()
            wait("scalar", sv, zmark)
            wait("scalar", dsem, dval)
            wait("scalar", sc, LASTSQ[0])
            em("scalar", lambda e, f=src_fn: e.activation(A(B_SQSCR, 2048), f(), ACT.Square,
                                                          accum_out=F(F_SSQ + jr, 1)).then_inc(sc.h, 1))
            qmark = sc.inc()
            LASTSQ[0] = qmark
            return rmark, qmark

        def p_finish(jr, src_fn, out_fn, rmark, qmark, apply_gate_st, apply_gate_sv=0):
            wait("vector", sv, rmark)
            em("vector", lambda e: e.tensor_scalar(F(F_NEGM + jr, 1), F(F_RS + jr, 1), -1.0 / D,
                                                   None, OP.mult).then_inc(sv.h, 1))
            nmark = sv.inc()
            wait("vector", sv, nmark)
            em("vector", lambda e: e.tensor_tensor(F(F_ST + jr, 1), F(F_NEGM + jr, 1),
                                                   F(F_NEGM + jr, 1), OP.mult).then_inc(sv.h, 1))
            mmark = sv.inc()
            wait("vector", sv, mmark)
            wait("vector", sc, qmark)
            em("vector", lambda e: e.scalar_tensor_tensor(F(F_VARP + jr, 1), F(F_SSQ + jr, 1),
                                                          1.0 / D, F(F_ST + jr, 1), OP.mult,
                                                          OP.subtract).then_inc(sv.h, 1))
            vmark = sv.inc()
            wait("scalar", sv, vmark)
            em("scalar", lambda e: e.activation(F(F_ST + jr, 1), F(F_VARP + jr, 1), ACT.Sqrt,
                                                bias=F(F_CEPS, 1)).then_inc(sc.h, 1))
            smark = sc.inc()
            wait("vector", sc, smark)
            em("vector", lambda e: e.reciprocal(F(F_RSTD + jr, 1), F(F_ST + jr, 1)).then_inc(sv.h, 1))
            rcm = sv.inc()
            wait("vector", sv, rcm)
            if apply_gate_st:
                wait("vector", st, apply_gate_st)
            if apply_gate_sv:
                wait("vector", sv, apply_gate_sv)
            em("vector", lambda e, f=src_fn, o=out_fn: e.tensor_scalar(
                o(), f(), F(F_NEGM + jr, 1), F(F_RSTD + jr, 1), OP.add, OP.mult)
                .then_inc(sv.h, 1))
            return sv.inc()

        def p_transpose16(xln_off, dest_fn, amark):
            wait("tensor", sv, amark)
            for r in range(4):
                half = r % 2
                if len(tp_gate2) >= 2:
                    geng, gm = tp_gate2[-2]
                    wait("tensor", sv if geng == "v" else sc, gm)
                for cth in range(4):
                    dc = 4 * r + cth
                    last = cth == 3
                    def mk(e, _dc=dc, _half=half, _cth=cth, _xo=xln_off, _last=last):
                        dstv = pA[:, _half * 512:(_half + 1) * 512].bitcast(BF16)
                        ins = e.transpose(dstv[:, _cth * 128:(_cth + 1) * 128],
                                          A(_xo + _dc * 128, 128), A(B_IDENT, 128))
                        if _last:
                            ins.then_inc(st.h, 1)
                    em("tensor", mk)
                tmark = st.inc()
                wait("vector", st, tmark)
                em("vector", lambda e, _half=half, _r=r, _dfn=dest_fn: e.tensor_copy(
                    _dfn(_r), pA[:, _half * 512:(_half + 1) * 512].bitcast(BF16)[:, 0:512])
                    .then_inc(sv.h, 1))
                tp_gate2.append(("v", sv.inc()))

        def XB_AP(i):
            b = i % 3
            if b < 2:
                return F(F_XB + b * 2048, 2048)
            return F(F_ACCG, 2048)

        def XB4(i):
            b = i % 4
            if b < 2:
                return F(F_XB + b * 2048, 2048)
            if b == 2:
                return F(F_ACCG, 2048)
            return F(F_QCOS, 2048)

        last_tp_st = 0
        pendq = []

        def flush_one_local():
            _j, _jr, _sf, _rm, _qm = pendq.pop(0)
            global_unused = None
            am = p_finish(_jr, _sf, (lambda: A(B_XLN, 2048)), _rm, _qm, last_tp_st)
            apply_marks[("L", _j)] = am
            dest_fn = (lambda __j=_j: (lambda r: A3(B_XLNT_LOC + (4 * r) * 1024 + __j * 128,
                                                    1024, 4, 128)))()
            p_transpose16(B_XLN, dest_fn, am)
            return st.n

        for j in range(NT_L):
            jr = j % 4
            if j == 2:
                wait("sync", st, t_bias_done)
            if j >= 3:
                wait("sync", sv, apply_marks[("L", j - 3)])
            _S = sdt[j % 3]
            em("sync", lambda e, _j=j, _a=XB_AP(j), _s=_S.h: e.dma_start(
                out=_a, in_=bass.AP(x_loc, _j * 128 * D, [[D, 128], [1, D]])).then_inc(_s, 16))
            dmark = (_S, _S.inc(16))
            src_fn = (lambda _j=j: (lambda: XB_AP(_j)))()
            rmark, qmark = p_stats(jr, src_fn, dmark)
            pendq.append((j, jr, src_fn, rmark, qmark))
            if len(pendq) > 2:
                last_tp_st = flush_one_local()
        while pendq:
            last_tp_st = flush_one_local()
        geng, gm = tp_gate2[-1]
        v_xlnt_loc_e = (geng, gm)
        geng2, gm2 = tp_gate2[-2]
        v_xlnt_loc = gm if geng == "v" else gm2


        qps = {0: pA, 1: pB}
        rope_q_marks = []
        qmm_marks = []
        for oc in range(NDC):
            if oc >= 4:
                wait("gpsimd", st, qmm_marks[oc - 4])
            for half in range(2):
                em("gpsimd", lambda e, _oc=oc, _h=half, _s=sdqa[oc % 4].h: e.dma_start(
                    out=A3(B_WCOL + (_oc % 4) * 2048 + _h * 8 * 128, 128, 8, 128),
                    in_=bass.AP(wq, (_h * 8 * 128) * D + _oc * 128,
                                [[D, 128], [128 * D, 8], [1, 128]])).then_inc(_s, 16))
                sdqa[oc % 4].inc(16)
            qdma_mark = sdqa[oc % 4].n
            P = qps[oc % 2]
            wait("tensor", sdqa[oc % 4], qdma_mark)
            if oc == 0:
                last_s_evac = max((m for g_, m in tp_gate2 if g_ == "s"), default=0)
                last_v_evac = max((m for g_, m in tp_gate2 if g_ == "v"), default=0)
                wait("tensor", sc, last_s_evac)
                wait("tensor", sv, last_v_evac)
                wait("tensor", sv, bias_done_marks[-1])
            if oc >= 2:
                wait("tensor", sv, rope_q_marks[oc - 2])
            for dc in range(NDC):
                for half in range(2):
                    last = dc == NDC - 1 and half == 1
                    def mk(e, _P=P, _oc=oc, _dc=dc, _h=half, _last=last):
                        ins = e.matmul(_P[:, _h * 512:(_h + 1) * 512],
                                       A(B_WCOL + (_oc % 4) * 2048 + _dc * 128, 128),
                                       A(B_XLNT_LOC + _dc * 1024 + _h * 512, 512),
                                       start=(_dc == 0), stop=(_dc == NDC - 1))
                        if _last:
                            ins.then_inc(st.h, 1)
                    em("tensor", mk)
            qmm_marks.append(st.inc())
            wait("vector", st, qmm_marks[oc])
            if oc == 0:
                wait("vector", sv, v_qtab)
                wait("vector", sv, v_bsh)
            for half in range(2):
                hw = 512
                base = half * 512
                em("vector", lambda e, _P=P, _oc=oc, _b=base: e.scalar_tensor_tensor(
                    F(F_RTMP + _b, 512), _P[:, _b:_b + 512], F(F_BQ + _oc, 1), F(F_QCOS + _b, 512),
                    OP.add, OP.mult))
                em("vector", lambda e, _P=P, _oc=oc, _b=base: e.scalar_tensor_tensor(
                    F(F_RU + _b, 512, 0, 64), _P[64:128, _b:_b + 512], F(F_BQSH + _oc, 1, 0, 64),
                    F(F_QSIN + _b, 512, 0, 64), OP.add, OP.mult))
                em("vector", lambda e, _P=P, _oc=oc, _b=base: e.scalar_tensor_tensor(
                    F(F_RU + _b, 512, 64, 64), _P[0:64, _b:_b + 512], F(F_BQSH + _oc, 1, 64, 64),
                    F(F_QSIN + _b, 512, 64, 64), OP.add, OP.mult).then_inc(sv.h, 1))
                _mq = sv.inc()
                wait("vector", sv, _mq)
                em("vector", lambda e, _oc=oc, _b=base: e.tensor_add(
                    A(B_QT + _oc * 1024 + _b, 512), F(F_RTMP + _b, 512), F(F_RU + _b, 512))
                    .then_inc(sv.h, 1))
                sv.inc()
            rope_q_marks.append(sv.n)

        tp_gate2.append(("v", rope_q_marks[-1]))
        tp_gate2.append(("v", rope_q_marks[-1]))
        kv_rope_marks = []
        vtok_marks = []
        vt_gate = []
        kv_pend = []
        kvmm_marks = {}
        XG = lambda _g: B_XLNT_GRP if _g % 2 == 0 else (B_XLNT_LOC + 8192)

        def queue_kv(g):
            evac_ready = tp_gate2[-1][1]
            for ch in range(4):
                def emit_chunk(_g=g, _ch=ch, _ev=evac_ready):
                    if _ch == 0:
                        wait("tensor", sv, _ev)
                        if _g == 0:
                            wait("tensor", sv, rope_q_marks[-1])
                            wait("tensor", sdwk, w_small)
                        else:
                            wait("tensor", sv, kv_rope_marks[-1])
                    for dc in range(_ch * 4, _ch * 4 + 4):
                        last = dc == NDC - 1
                        for c in range(2):
                            def mkk(e, _dc=dc, _c=c, _gg=_g):
                                e.matmul(pB[:, _c * 512:(_c + 1) * 512],
                                         A(B_WK + _dc * 256 + _c * 128, 128),
                                         A(XG(_gg) + _dc * 512, 512),
                                         start=(_dc == 0), stop=(_dc == NDC - 1))
                            em("tensor", mkk)
                        for c in range(2):
                            def mkv(e, _dc=dc, _c=c, _gg=_g, _last=last):
                                ins = e.matmul(pC[:, _c * 512:(_c + 1) * 512],
                                               A(B_WV + _dc * 256 + _c * 128, 128),
                                               A(XG(_gg) + _dc * 512, 512),
                                               start=(_dc == 0), stop=(_dc == NDC - 1))
                                if _last and _c == 1:
                                    ins.then_inc(st.h, 1)
                            em("tensor", mkv)
                    if _ch == 3:
                        kvmm_marks[_g] = st.inc()
                kv_pend.append(emit_chunk)

        def flush_pend():
            _i, _jr, _sf, _rm, _qm = pend.pop(0)
            _g = _i // 4
            if _i % 4 == 0 and _g >= 2:
                wait("vector", st, kvmm_marks[_g - 2])
            xoff = B_XLN if _i % 2 == 0 else B_XLNT_LOC
            agate = last_tp2[_i % 2]
            if _i % 2 == 1 and not xln2_init[0]:
                agate = 0
                wait("vector", st, qmm_marks[-1])
                xln2_init[0] = True
            am = p_finish(_jr, _sf, (lambda _x=xoff: (lambda: A(_x, 2048)))(), _rm, _qm, agate)
            apply_marks[("F", _i)] = am
            _tt = _i % 4
            dest_fn = (lambda __tt=_tt, __g=_g: (lambda r: A3(
                XG(__g) + (4 * r) * 512 + __tt * 128, 512, 4, 128)))()
            p_transpose16(xoff, dest_fn, am)
            last_tp2[_i % 2] = st.n

        def rope_vtrans(g):
            wait("vector", st, kvmm_marks[g])
            if g == 0:
                wait("vector", sv, v_ktabs)
            for c in range(2):
                em("vector", lambda e, _c=c, _g=g: e.scalar_tensor_tensor(
                    F(F_RTMP + _c * 512, 512), pB[:, _c * 512:(_c + 1) * 512], F(F_BK + _c, 1),
                    A(B_KT + _g * 512, 512), OP.add, OP.mult))
                em("vector", lambda e, _c=c, _g=g: e.scalar_tensor_tensor(
                    F(F_RU + _c * 512, 512, 0, 64), pB[64:128, _c * 512:(_c + 1) * 512],
                    F(F_BKSH + _c, 1, 0, 64),
                    bass.AP(ab, B_KT + 4096 + _g * 512, [[NB, 64], [1, 512]]), OP.add, OP.mult))
                def mkru(e, _c=c, _g=g):
                    ins = e.scalar_tensor_tensor(
                        F(F_RU + _c * 512, 512, 64, 64), pB[0:64, _c * 512:(_c + 1) * 512],
                        F(F_BKSH + _c, 1, 64, 64),
                        bass.AP(ab, 64 * NB + B_KT + 4096 + _g * 512, [[NB, 64], [1, 512]]),
                        OP.add, OP.mult)
                    if _c == 1:
                        ins.then_inc(sv.h, 1)
                em("vector", mkru)
            src_mark = sv.inc()
            wait("vector", sv, src_mark)
            for c in range(2):
                em("vector", lambda e, _c=c, _g=g: e.tensor_add(
                    A(B_KT + _c * 4096 + _g * 512, 512), F(F_RTMP + _c * 512, 512),
                    F(F_RU + _c * 512, 512)).then_inc(sv.h, 1))
                sv.inc()
                em("vector", lambda e, _c=c: e.tensor_scalar(
                    A(B_VTSCR + _c * 512, 512), pC[:, _c * 512:(_c + 1) * 512], F(F_BV + _c, 1),
                    None, OP.add).then_inc(sv.h, 1))
                sv.inc()
            kv_rope_marks.append(sv.n)

            wait("tensor", sv, kv_rope_marks[-1])
            for r in range(2):
                half = r % 2
                if len(vt_gate) >= 2:
                    wait("tensor", sv, vt_gate[-2])
                for q4 in range(4):
                    sidx = r * 2 + q4 // 2
                    c = q4 % 2
                    last = q4 == 3
                    def mkt(e, _s=sidx, _c=c, _half=half, _last=last):
                        dstv = pD[:, _half * 512:(_half + 1) * 512].bitcast(BF16)
                        ins = e.transpose(dstv[:, (2 * (_s % 2) + _c) * 128:(2 * (_s % 2) + _c) * 128 + 128],
                                          A(B_VTSCR + _c * 512 + _s * 128, 128), A(B_IDENT, 128))
                        if _last:
                            ins.then_inc(st.h, 1)
                    em("tensor", mkt)
                tmark = st.inc()
                wait("vector", st, tmark)
                em("vector", lambda e, _g=g, _r=r, _half=half: e.tensor_copy(
                    A(B_VTOK + (_g * 4 + _r * 2) * 256, 512),
                    pD[:, _half * 512:(_half + 1) * 512].bitcast(BF16)[:, 0:512]).then_inc(sv.h, 1))
                vt_gate.append(sv.inc())
            vtok_marks.append(vt_gate[-1])

        pend = []
        last_tp2 = [last_tp_st, 0]
        xln2_init = [False]
        for g in range(NGRP):
            for tt in range(4):
                i = g * 4 + tt
                jr = i % 4
                if i >= 4:
                    gate = apply_marks.get(("F", i - 4))
                elif i == 3:
                    gate = None
                    wait("sync", sv, rope_q_marks[-1])
                else:
                    gate = apply_marks.get(("L", {0: 6, 1: 7, 2: 5}[i]))
                if gate:
                    wait("sync", sv, gate)
                _S = sdt[i % 4]
                em("sync", lambda e, _i=i, _a=XB4(i), _s=_S.h: e.dma_start(
                    out=_a, in_=bass.AP(x_full, _i * 128 * D, [[D, 128], [1, D]])).then_inc(_s, 16))
                dmark = (_S, _S.inc(16))
                src_fn = (lambda _i=i: (lambda: XB4(_i)))()
                rmark, qmark = p_stats(jr, src_fn, dmark)
                pend.append((i, jr, src_fn, rmark, qmark))
                if len(pend) > 2:
                    flush_pend()
                    if kv_pend:
                        kv_pend.pop(0)()
            while pend:
                flush_pend()
                if kv_pend:
                    kv_pend.pop(0)()
            while kv_pend:
                kv_pend.pop(0)()
            if g >= 1:
                rope_vtrans(g - 1)
            queue_kv(g)
        while kv_pend:
            kv_pend.pop(0)()
        rope_vtrans(NGRP - 1)

        v_A_done = sv.n
        kvmm_last = kvmm_marks[NGRP - 1]
        if _DBG:
            wait("sync", sv, v_A_done)
            em("sync", lambda e: e.dma_start(
                out=bass.AP(dbg, 0, [[8192, 128], [1, 8192]]),
                in_=bass.AP(ab, B_QT, [[NB, 128], [1, 16384]]).bitcast(F32)).then_inc(sdo.h, 16))
            sdo.inc(16)
            em("sync", lambda e: e.dma_start(
                out=bass.AP(dbg, 1048576, [[4096, 128], [1, 4096]]),
                in_=bass.AP(ab, B_KT, [[NB, 128], [1, 8192]]).bitcast(F32)).then_inc(sdo.h, 16))
            sdo.inc(16)
            em("sync", lambda e: e.dma_start(
                out=bass.AP(dbg, 1572864, [[4096, 128], [1, 4096]]),
                in_=bass.AP(ab, B_VTOK, [[NB, 128], [1, 8192]]).bitcast(F32)).then_inc(sdo.h, 16))
            sdo.inc(16)

        wait("sync", sv, v_A_done)
        xloc_marks = {}
        for t in range(6):
            em("sync", lambda e, _t=t: e.dma_start(
                out=F(F_X1 + _t * 2048, 2048),
                in_=bass.AP(x_loc, _t * 128 * D, [[D, 128], [1, D]])).then_inc(sdxl.h, 16))
            xloc_marks[t] = sdxl.inc(16)

        exp_marks = []
        pvsum_marks = []
        score_marks = []
        recip_marks = []
        norm_marks = []
        NKC = 32
        NKT = H * NKC

        SSETS = None

        def emit_scores(j):
            hj, kcj = divmod(j, NKC)
            P = SSETS[j % 3]
            if j >= 3:
                wait("tensor", sc, exp_marks[j - 3])
            if j == 0:
                wait("tensor", sv, tp_gate2[-1][1])
            elif j == 2:
                wait("tensor", sv, vt_gate[-1])
            for half in range(2):
                last = half == 1
                def mks(e, _P=P, _h=hj, _g=hj // 8, _kc=kcj, _half=half, _last=last):
                    ins = e.matmul(_P[:, _half * 512:(_half + 1) * 512],
                                   A(B_KT + _g * 4096 + _kc * 128, 128),
                                   A(B_QT + _h * 1024 + _half * 512, 512),
                                   start=True, stop=True)
                    if _last:
                        ins.then_inc(st.h, 1)
                em("tensor", mks)
            score_marks.append(st.inc())

        acc_eng = {kc: "vector" for kc in range(NKC)}
        acc_marks = {}
        summ_marks = []
        denb_marks = []
        denb_half_marks = []
        PCEV = bass.AP(ab, B_XLN, [[NB, 128], [1, 2048]]).bitcast(F32)
        SSETS = [pA, pB, pD]
        wait("vector", st, kvmm_last)
        em("vector", lambda e: e.memset(A(B_ONESBF, 128), 1.0).then_inc(sv.h, 1))
        ones_bf_mark = sv.inc()
        emit_scores(0)
        next_sc = [1]
        epi_pend = []
        for h in range(H):
            g = h // 8
            for kc in range(NKC):
                k = h * NKC + kc
                P = SSETS[k % 3]
                while epi_pend and epi_pend[0][0] == kc:
                    epi_pend.pop(0)[1]()
                wait("scalar", st, score_marks[k])
                if k >= 6:
                    kind, mv = acc_marks[k - 6]
                    if kind == "v":
                        wait("scalar", sv, mv)
                    elif kind == "t":
                        wait("scalar", st, mv)
                    else:
                        wait("scalar", sg, mv)
                em("scalar", lambda e, _P=P, _o=PROBS_OFF(k): e.activation(
                    A(_o, 1024), _P[:, 0:1024], ACT.Exp).then_inc(sc.h, 1))
                exp_marks.append(sc.inc())
                tgt = min(k + 3 if kc == NKC - 1 else k + 1, NKT - 1)
                while next_sc[0] <= tgt:
                    emit_scores(next_sc[0])
                    next_sc[0] += 1
                wait("tensor", sc, exp_marks[k])
                eng = acc_eng[kc]
                if kc >= NKC - 2:
                    acc_marks[k] = ("t", None)
                elif kc <= 1:
                    wait("vector", sc, exp_marks[k])
                    if h >= 1:
                        wait("vector", st, summ_marks[h - 1])
                    _dst = F_ACCV if kc == 0 else F_ACCG
                    em("vector", lambda e, _o=PROBS_OFF(k), _d=_dst: e.tensor_copy(
                        F(_d, 1024), A(_o, 1024)).then_inc(sv.h, 1))
                    acc_marks[k] = ("v", sv.inc())
                    if kc == 1:
                        seed_mark = acc_marks[k][1]
                elif eng == "vector":
                    wait("vector", sc, exp_marks[k])
                    if kc % 2 == 0:
                        acc_marks[k] = ("v", acc_marks[k - 1][1])
                    else:
                        wait("vector", sc, exp_marks[k - 1])
                        if kc == 3:
                            wait("vector", sv, seed_mark)
                        par = ((kc - 3) // 2) % 2
                        _pr = B_PRPAIR if par == 0 else B_PRPAIR2
                        _acc = F_ACCV if par == 0 else F_ACCG
                        em("vector", lambda e, _a=PROBS_OFF(k - 1), _b=PROBS_OFF(k),
                           _p=_pr: e.tensor_tensor(
                            A(_p, 1024), A(_a, 1024), A(_b, 1024), OP.add).then_inc(sv.h, 1))
                        _pm = sv.inc()
                        wait("vector", sv, _pm)
                        em("vector", lambda e, _c=_acc, _p=_pr: e.tensor_tensor(
                            F(_c, 1024), F(_c, 1024), A(_p, 1024), OP.add).then_inc(sv.h, 1))
                        acc_marks[k] = ("v", sv.inc())
                        acc_marks[k - 1] = ("v", acc_marks[k][1])
                else:
                    wait("gpsimd", sc, exp_marks[k])
                    if kc == 1 and h >= 1:
                        wait("gpsimd", st, summ_marks[h - 1])
                    if kc == 1:
                        em("gpsimd", lambda e, _o=PROBS_OFF(k): e.tensor_copy(
                            F(F_ACCG, 1024), A(_o, 1024)).then_inc(sg.h, 1))
                    else:
                        em("gpsimd", lambda e, _o=PROBS_OFF(k): e.tensor_tensor(
                            F(F_ACCG, 1024), F(F_ACCG, 1024), A(_o, 1024), OP.add).then_inc(sg.h, 1))
                    acc_marks[k] = ("g", sg.inc())
                for half in range(2):
                    if kc == 0 and h >= 1:
                        wait("tensor", sv, denb_half_marks[h - 1][half])
                    elif kc == 0 and h == 0 and half == 0:
                        wait("tensor", sv, kv_rope_marks[-1])
                    last = half == 1
                    def mkpv(e, _kc=kc, _g=g, _half=half, _last=last, _o=PROBS_OFF(k)):
                        ins = e.matmul(pC[:, _half * 512:(_half + 1) * 512],
                                       A(B_VTOK + _kc * 256 + _g * 128, 128),
                                       A(_o + _half * 512, 512),
                                       start=(_kc == 0), stop=(_kc == NKC - 1))
                        if _last:
                            ins.then_inc(st.h, 1)
                    em("tensor", mkpv)
                pvsum_marks.append(st.inc())
            last_v = max(m for kk, (kind, m) in acc_marks.items()
                         if kind == "v" and h * NKC <= kk < (h + 1) * NKC)
            wait("scalar", st, pvsum_marks[-1])
            if norm_marks:
                wait("scalar", sv, norm_marks[-1])
            pcev_sc_h = []
            for half in range(2):
                em("scalar", lambda e, _h=half: e.activation(
                    PCEV[:, _h * 512:(_h + 1) * 512], pC[:, _h * 512:(_h + 1) * 512],
                    ACT.Copy).then_inc(sc.h, 1))
                pcev_sc_h.append(sc.inc())
            wait("vector", sv, last_v)
            em("vector", lambda e: e.tensor_tensor(F(F_ACCV, 1024), F(F_ACCV, 1024),
                                                   F(F_ACCG, 1024), OP.add).then_inc(sv.h, 1))
            merge_mark = sv.inc()
            wait("tensor", sv, merge_mark)
            if h == 0:
                wait("tensor", sv, ones_bf_mark)
            k30 = h * NKC + NKC - 2
            summ_h = []
            for half in range(2):
                wait("tensor", sc, pcev_sc_h[half])
                em("tensor", lambda e, _half=half, _o=PROBS_OFF(k30): e.matmul(
                    pC[:, _half * 512:(_half + 1) * 512], A(B_ONESBF, 128),
                    A(_o + _half * 512, 512), start=True, stop=False))
                em("tensor", lambda e, _half=half, _o=PROBS_OFF(k30 + 1): e.matmul(
                    pC[:, _half * 512:(_half + 1) * 512], A(B_ONESBF, 128),
                    A(_o + _half * 512, 512), start=False, stop=False))
                def mksum(e, _half=half):
                    e.matmul(pC[:, _half * 512:(_half + 1) * 512], F(F_ONESROW, 128),
                             F(F_ACCV + _half * 512, 512), start=False, stop=True) \
                        .then_inc(st.h, 1)
                em("tensor", mksum)
                summ_h.append(st.inc())
            summ_marks.append(summ_h[1])
            acc_marks[k30] = ("t", summ_marks[-1])
            acc_marks[k30 + 1] = ("t", summ_marks[-1])
            denb_h = []
            for half in range(2):
                wait("vector", st, summ_h[half])
                em("vector", lambda e, _h=half: e.tensor_copy(
                    F(F_RECIPB + _h * 512, 512), pC[:, _h * 512:(_h + 1) * 512])
                   .then_inc(sv.h, 1))
                denb_h.append(sv.inc())
            denb_half_marks.append(denb_h)
            denb_marks.append(denb_h[1])
            rmk = {}
            _dmk = denb_marks[-1]
            _hcap = h

            def _mk_recip(_o, _w, _dmk=_dmk, _rmk=rmk):
                def f():
                    wait("vector", sv, _dmk)
                    em("vector", lambda e: e.reciprocal(F(F_RECIPB + _o, _w),
                                                        F(F_RECIPB + _o, _w))
                       .then_inc(sv.h, 1))
                    _rmk[_o] = sv.inc()
                return f

            def _mk_norm(_o, _hh=_hcap, _rmk=rmk):
                def f():
                    wait("vector", sv, _rmk[_o + 256])
                    em("vector", lambda e: e.tensor_tensor(
                        A(B_ATTNT + _hh * 1024 + _o, 512), PCEV[:, _o:_o + 512],
                        F(F_RECIPB + _o, 512), OP.mult).then_inc(sv.h, 1))
                    norm_marks.append(sv.inc())
                return f

            epi_pend.extend([(5, _mk_recip(0, 256)), (9, _mk_recip(256, 256)),
                             (13, _mk_recip(512, 256)), (17, _mk_recip(768, 256)),
                             (19, _mk_norm(0)), (21, _mk_norm(512))])
        while epi_pend:
            epi_pend.pop(0)[1]()
        v_B_done = sv.n
        if _DBG:
            wait("sync", sv, v_B_done)
            em("sync", lambda e: e.dma_start(
                out=bass.AP(dbg, 2097152, [[8192, 128], [1, 8192]]),
                in_=bass.AP(ab, B_ATTNT, [[NB, 128], [1, 16384]]).bitcast(F32)).then_inc(sdo.h, 16))
            sdo.inc(16)

        wait("sync", sv, v_B_done)
        for t in (6, 7):
            em("sync", lambda e, _t=t: e.dma_start(
                out=F(F_X1 + _t * 2048, 2048),
                in_=bass.AP(x_loc, _t * 128 * D, [[D, 128], [1, D]])).then_inc(sdxl.h, 16))
            xloc_marks[t] = sdxl.inc(16)

        ops_sets = {0: pA, 1: pB}
        wo_mm_marks = []
        osb_marks = []
        add_marks = []
        scan_marks = []
        fw_dma_marks = []
        t2last = {}

        def OSB_AP(i, off=0, w=1024):
            if i % 2 == 0:
                return F(F_OSB + off, w)
            return bass.AP(ab, B_PROBS + 2 * off, [[NB, 128], [1, 2 * w]]).bitcast(F32)

        def c_emit_T(i):
            wait("tensor", sv, osb_marks[i])
            for r in range(2):
                if len(add_marks) >= 2:
                    wait("tensor", sv, add_marks[-2])
                for jt in range(4):
                    tsub = r * 4 + jt
                    last = jt == 3
                    def mkt(e, _i=i, _r=r, _jt=jt, _tsub=tsub, _last=last):
                        ins = e.transpose(
                            pC[:, _r * 512 + _jt * 128:_r * 512 + (_jt + 1) * 128],
                            OSB_AP(_i, _tsub * 128, 128), F(F_IDENTF, 128))
                        if _last:
                            ins.then_inc(st.h, 1)
                    em("tensor", mkt)
                tmark = st.inc()
                wait("vector", st, tmark)
                if i == 0 and r == 0:
                    wait("vector", sdxl, 128)
                def mka(e, _i=i, _r=r):
                    dst = F3(F_X1 + (_r * 4) * 2048 + _i * 128, 2048, 4, 128)
                    e.tensor_tensor(dst, pC[:, _r * 512:_r * 512 + 512], dst, OP.add) \
                        .then_inc(sv.h, 1)
                em("vector", mka)
                add_marks.append(sv.inc())
            t2last[i] = st.n

        for oc in range(NDC):
            if oc >= 2:
                wait("gpsimd", st, wo_mm_marks[oc - 2])
            elif oc == 0:
                wait("gpsimd", st, kvmm_last)
            for half in range(2):
                em("gpsimd", lambda e, _oc=oc, _h=half, _s=sdq[oc % 2].h: e.dma_start(
                    out=A3(B_WCOL + (_oc % 2) * 2048 + _h * 8 * 128, 128, 8, 128),
                    in_=bass.AP(wo, (_h * 8 * 128) * D + _oc * 128,
                                [[D, 128], [128 * D, 8], [1, 128]])).then_inc(_s, 16))
                sdq[oc % 2].inc(16)
            wdma = sdq[oc % 2].n
            if oc >= 2:
                wait("sync", sv, scan_marks[oc - 2])
            em("sync", lambda e, _i=oc, _s=sdf[oc % 2].h: e.dma_start(
                out=bass.AP(ab, B_FWB + (_i % 2) * 4096, [[NB, 128], [1, 4096]]).bitcast(F32),
                in_=bass.AP(fw, _i * 128 * D, [[D, 128], [1, D]])).then_inc(_s, 16))
            fw_dma_marks.append((sdf[oc % 2], sdf[oc % 2].inc(16)))
            P = ops_sets[oc % 2]
            wait("tensor", sdq[oc % 2], wdma)
            if oc == 0:
                wait("tensor", sv, v_B_done)
            if oc >= 2:
                wait("tensor", sv, osb_marks[oc - 2])
            for dc in range(NDC):
                for half in range(2):
                    last = dc == NDC - 1 and half == 1
                    def mko(e, _P=P, _oc=oc, _dc=dc, _h=half, _last=last):
                        ins = e.matmul(_P[:, _h * 512:(_h + 1) * 512],
                                       A(B_WCOL + (_oc % 2) * 2048 + _dc * 128, 128),
                                       A(B_ATTNT + _dc * 1024 + _h * 512, 512),
                                       start=(_dc == 0), stop=(_dc == NDC - 1))
                        if _last:
                            ins.then_inc(st.h, 1)
                    em("tensor", mko)
            wo_mm_marks.append(st.inc())
            wait("vector", st, wo_mm_marks[oc])
            if oc >= 2:
                wait("vector", st, t2last[oc - 2])
            em("vector", lambda e, _P=P, _oc=oc: e.tensor_scalar(
                OSB_AP(_oc), _P[:, 0:1024], F(F_BO + _oc, 1), None, OP.add).then_inc(sv.h, 1))
            osb_marks.append(sv.inc())
            wait("vector", fw_dma_marks[oc][0], fw_dma_marks[oc][1])
            em("vector", lambda e, _i=oc: e.reduce_sum(
                F(F_WABS + _i, 1), bass.AP(ab, B_FWB + (_i % 2) * 4096,
                                           [[NB, 128], [1, 4096]]).bitcast(F32)[:, 0:2048],
                axis=AX, apply_absolute_value=True).then_inc(sv.h, 1))
            scan_marks.append(sv.inc())
            if oc >= 1:
                c_emit_T(oc - 1)
        c_emit_T(NDC - 1)
        v_C_done = sv.n

        if _DBG:
            wait("sync", sv, v_C_done)
            em("sync", lambda e: e.dma_start(
                out=bass.AP(dbg, 4194304, [[16384, 128], [1, 16384]]),
                in_=F(F_X1, 16384)).then_inc(sdo.h, 16))
            sdo.inc(16)

        wait("vector", sv, scan_marks[-1])
        em("vector", lambda e: e.reduce_sum(F(F_WABS1, 1), F(F_WABS, 16), axis=AX).then_inc(sv.h, 1))
        m = sv.inc()
        wait("vector", sv, m)
        em("vector", lambda e: e.tensor_copy(F(F_SPACER, 1), F(F_ONESP, 1)).then_inc(sv.h, 1))
        v_wabs = sv.inc()
        ONESROW_C = lambda: bass.AP(ab, B_VTSCR, [[NB, 1], [1, 256]]).bitcast(F32)[:, 0:128]
        THROW_C = lambda: bass.AP(ab, B_VTSCR + 256, [[NB, 1], [1, 2]]).bitcast(F32)[:, 0:1]
        em("vector", lambda e: e.memset(ONESROW_C(), 1.0).then_inc(sv.h, 1))
        sv.inc()
        wait("tensor", sv, v_wabs)
        wait("tensor", sv, add_marks[-1])
        em("tensor", lambda e: e.matmul(pC[0:1, 0:1], F(F_WABS1, 1), F(F_ONESP, 1),
                                        start=True, stop=True).then_inc(st.h, 1))
        t_tot = st.inc()
        wait("vector", st, t_tot)
        em("vector", lambda e: e.tensor_copy(THROW_C(), pC[0:1, 0:1]).then_inc(sv.h, 1))
        m = sv.inc()
        wait("vector", sv, m)
        em("vector", lambda e: e.tensor_scalar(THROW_C(), THROW_C(),
                                               0.5 / (float(D) * float(D)), 0.5 * QEPS,
                                               OP.mult, OP.add).then_inc(sv.h, 1))
        v_throw = sv.inc()
        wait("tensor", sv, v_throw)
        em("tensor", lambda e: e.matmul(pC[:, 0:1], ONESROW_C(), THROW_C(),
                                        start=True, stop=True).then_inc(st.h, 1))
        t_thr = st.inc()
        wait("vector", st, t_thr)
        em("vector", lambda e: e.tensor_copy(F(F_THR, 1), pC[:, 0:1]))
        em("vector", lambda e: e.tensor_scalar(F(F_NTHR, 1), pC[:, 0:1], -1.0, None, OP.mult)
           .then_inc(sv.h, 1))
        v_thr = sv.inc()

        xn_tp_marks = []
        wait("scalar", st, t2last[NDC - 1])
        YB = lambda par: bass.AP(ab, B_Y + par * 4096, [[NB, 128], [1, 4096]]).bitcast(F32)
        d1s = {}
        d1f = {}
        d2s = {}
        d2a = {}

        def d2_finish_T(t):
            am = p_finish(2 + t % 2, (lambda _t=t: (lambda: YB(_t % 2)))(),
                          (lambda: A(B_XLN, 2048)), d2s[t][0], d2s[t][1],
                          apply_gate_st=last_tp_st2[0])
            d2a[t] = am
            dest_fn = (lambda _t=t: (lambda r: A3(B_XN2T + (4 * r) * 1024 + _t * 128,
                                                  1024, 4, 128)))()
            p_transpose16(B_XLN, dest_fn, am)
            last_tp_st2[0] = st.n

        last_tp_st2 = [last_tp_st]
        for t in range(NT_L):
            src_fn = (lambda _t=t: (lambda: F(F_X1 + _t * 2048, 2048)))()
            d1s[t] = p_stats(t % 2, src_fn, (sv, v_C_done))
            if t >= 1:
                tm = t - 1
                yre = d2a.get(tm - 2, 0)
                d1f[tm] = p_finish(tm % 2, (lambda _t=tm: (lambda: F(F_X1 + _t * 2048, 2048)))(),
                                   (lambda _t=tm: (lambda: YB(_t % 2)))(), d1s[tm][0], d1s[tm][1],
                                   apply_gate_st=0, apply_gate_sv=yre)
                d2s[tm] = p_stats(2 + tm % 2, (lambda _t=tm: (lambda: YB(_t % 2)))(),
                                  (sv, d1f[tm]))
            if t >= 2:
                d2_finish_T(t - 2)
        tm = NT_L - 1
        d1f[tm] = p_finish(tm % 2, (lambda _t=tm: (lambda: F(F_X1 + _t * 2048, 2048)))(),
                           (lambda _t=tm: (lambda: YB(_t % 2)))(), d1s[tm][0], d1s[tm][1],
                           apply_gate_st=0, apply_gate_sv=d2a.get(tm - 2, 0))
        d2s[tm] = p_stats(2 + tm % 2, (lambda _t=tm: (lambda: YB(_t % 2)))(), (sv, d1f[tm]))
        d2_finish_T(NT_L - 2)
        d2_finish_T(NT_L - 1)
        geng, gmv = tp_gate2[-1]
        wait("vector", sv if geng == "v" else sc, gmv)
        geng, gmv = tp_gate2[-2]
        wait("vector", sv if geng == "v" else sc, gmv)
        for dc in range(NDC):
            em("vector", lambda e, _dc=dc: e.tensor_scalar(
                A(B_XN2T + _dc * 1024, 1024), A(B_XN2T + _dc * 1024, 1024), F(F_GAM + _dc, 1),
                None, OP.mult).then_inc(sv.h, 1))
            sv.inc()
        v_D_done = sv.n
        if _DBG:
            wait("sync", sv, v_D_done)
            em("sync", lambda e: e.dma_start(
                out=bass.AP(dbg, 3145728, [[8192, 128], [1, 8192]]),
                in_=bass.AP(ab, B_XN2T, [[NB, 128], [1, 16384]]).bitcast(F32)).then_inc(sdo.h, 16))
            sdo.inc(16)

        quant_marks = []
        wqt_evac_marks = []
        cb_marks = []
        ffn_marks = []
        fsb_marks = []
        eadd_marks = []
        e_t2last = {}

        def e_emit_T2(i):
            wait("tensor", sv, fsb_marks[i])
            for r in range(2):
                if len(eadd_marks) >= 2:
                    wait("tensor", sv, eadd_marks[-2])
                elif i == 0 and r == 0:
                    wait("tensor", sv, add_marks[-1])
                    wait("tensor", sv, v_thr)
                for jt in range(4):
                    tsub = r * 4 + jt
                    last = jt == 3
                    def mkt2(e, _i=i, _r=r, _jt=jt, _tsub=tsub, _last=last):
                        ins = e.transpose(
                            pC[:, _r * 512 + _jt * 128:_r * 512 + (_jt + 1) * 128],
                            OSB_AP(_i, _tsub * 128, 128), F(F_IDENTF, 128))
                        if _last:
                            ins.then_inc(st.h, 1)
                    em("tensor", mkt2)
                tmark = st.inc()
                wait("vector", st, tmark)
                def mka2(e, _i=i, _r=r):
                    dst = F3(F_X1 + (_r * 4) * 2048 + _i * 128, 2048, 4, 128)
                    e.tensor_tensor(dst, pC[:, _r * 512:_r * 512 + 512], dst, OP.add) \
                        .then_inc(sv.h, 1)
                em("vector", mka2)
                eadd_marks.append(sv.inc())
            e_t2last[i] = st.n

        EFW = lambda _i: (B_FWB + (_i % 4) * 4096) if (_i % 4) < 2 else (B_WK + ((_i % 4) - 2) * 4096)
        Q1 = lambda: bass.AP(ab, B_QSCR, [[NB, 128], [1, 4096]]).bitcast(F32)
        Q2 = lambda: bass.AP(ab, B_QSCR + 4096, [[NB, 128], [1, 4096]]).bitcast(F32)
        e_dma_marks = []

        def e_dma(i):
            if i >= NDC:
                return
            if i >= 4:
                wait("sync", sv, quant_marks[i - 4])
            elif i < 2:
                wait("sync", sv, scan_marks[-1])
            em("sync", lambda e, _i=i, _s=sdf[i % 4].h: e.dma_start(
                out=bass.AP(ab, EFW(_i), [[NB, 128], [1, 4096]]).bitcast(F32),
                in_=bass.AP(fw, _i * 128 * D, [[D, 128], [1, D]])).then_inc(_s, 16))
            e_dma_marks.append(sdf[i % 4].inc(16))

        def e_quant(i):
            if i >= NDC:
                return
            wait("vector", sdf[i % 4], e_dma_marks[i])
            if i == 0:
                wait("vector", sv, v_thr)
            FWAP = (lambda _i=i: (lambda: bass.AP(ab, EFW(_i),
                                                  [[NB, 128], [1, 4096]]).bitcast(F32)))()
            em("vector", lambda e, f=FWAP: e.tensor_scalar(Q1(), f(), F(F_THR, 1), None, OP.is_gt))
            em("vector", lambda e, f=FWAP: e.tensor_scalar(Q2(), f(), F(F_NTHR, 1), None, OP.is_lt)
               .then_inc(sv.h, 1))
            _mq2 = sv.inc()
            wait("vector", sv, _mq2)
            em("vector", lambda e, _i=i: e.tensor_sub(A(B_WQROW + (_i % 2) * 2048, 2048),
                                                      Q1(), Q2()).then_inc(sv.h, 1))
            quant_marks.append(sv.inc())

        e_dma(0)
        e_quant(0)
        e_dma(1)
        for oc in range(NDC):
            wait("tensor", sv, quant_marks[oc])
            for r in range(4):
                half = r % 2
                if len(wqt_evac_marks) >= 2:
                    wait("tensor", sv, wqt_evac_marks[-2])
                elif r == 0 and oc == 0:
                    _ge, _gm = tp_gate2[-1]
                    wait("tensor", sv if _ge == "v" else sc, _gm)
                    _ge, _gm = tp_gate2[-2]
                    wait("tensor", sv if _ge == "v" else sc, _gm)
                for cth in range(4):
                    ic = 4 * r + cth
                    last = cth == 3
                    def mkq(e, _oc=oc, _ic=ic, _half=half, _cth=cth, _last=last):
                        dstv = pA[:, _half * 512:(_half + 1) * 512].bitcast(BF16)
                        ins = e.transpose(dstv[:, _cth * 128:(_cth + 1) * 128],
                                          A(B_WQROW + (_oc % 2) * 2048 + _ic * 128, 128),
                                          A(B_IDENT, 128))
                        if _last:
                            ins.then_inc(st.h, 1)
                    em("tensor", mkq)
                tmark = st.inc()
                wait("vector", st, tmark)
                em("vector", lambda e, _oc=oc, _r=r, _half=half: e.tensor_copy(
                    A(B_WQT + (_oc % 2) * 2048 + _r * 512, 512),
                    pA[:, _half * 512:(_half + 1) * 512].bitcast(BF16)[:, 0:512]).then_inc(sv.h, 1))
                wqt_evac_marks.append(sv.inc())
            e_quant(oc + 1)
            e_dma(oc + 2)
            wait("tensor", sv, wqt_evac_marks[-1])
            if oc >= 1:
                wait("tensor", sv, cb_marks[oc - 1])
            for ic in range(NDC):
                last = ic == NDC - 1
                def mkc(e, _oc=oc, _ic=ic, _last=last):
                    ins = e.matmul(pD[:, 0:1], A(B_WQT + (_oc % 2) * 2048 + _ic * 128, 128),
                                   A(B_BETBF + _ic, 1), start=(_ic == 0), stop=(_ic == NDC - 1))
                    if _last:
                        ins.then_inc(st.h, 1)
                em("tensor", mkc)
            cmark = st.inc()
            wait("vector", st, cmark)
            em("vector", lambda e, _oc=oc: e.tensor_scalar(F(F_CB + (_oc % 2), 1), pD[:, 0:1],
                                                           F(F_FB + _oc, 1), None, OP.add)
               .then_inc(sv.h, 1))
            cb_marks.append(sv.inc())
            if oc >= 1:
                wait("tensor", sv, fsb_marks[-1])
            else:
                wait("tensor", sv, osb_marks[-1])
                wait("tensor", sv, v_D_done)
            for ic in range(NDC):
                for half in range(2):
                    last = ic == NDC - 1 and half == 1
                    def mkf(e, _oc=oc, _ic=ic, _h=half, _last=last):
                        ins = e.matmul(pB[:, _h * 512:(_h + 1) * 512],
                                       A(B_WQT + (_oc % 2) * 2048 + _ic * 128, 128),
                                       A(B_XN2T + _ic * 1024 + _h * 512, 512),
                                       start=(_ic == 0), stop=(_ic == NDC - 1))
                        if _last:
                            ins.then_inc(st.h, 1)
                    em("tensor", mkf)
            ffn_marks.append(st.inc())
            wait("vector", st, ffn_marks[oc])
            wait("vector", sv, cb_marks[oc])
            if oc >= 2:
                wait("vector", st, e_t2last[oc - 2])
            em("vector", lambda e, _oc=oc: e.tensor_scalar(
                OSB_AP(_oc), pB[:, 0:1024], F(F_CB + (_oc % 2), 1), None, OP.add).then_inc(sv.h, 1))
            fsb_marks.append(sv.inc())
            if oc >= 1:
                e_emit_T2(oc - 1)
        e_emit_T2(NDC - 1)
        v_E_done = sv.n

        wait("sync", sv, v_E_done)
        for t in range(NT_L):
            em("sync", lambda e, _t=t: e.dma_start(
                out=bass.AP(out, _t * 128 * D, [[D, 128], [1, D]]),
                in_=F(F_X1 + _t * 2048, 2048)).then_inc(sdo.h, 16))
            sdo.inc(16)
        em("sync", lambda e, _v=sdo.n: e.wait_ge(sdo.h, _v))

        if _OPS_HOOK is not None:
            _OPS_HOOK(ops, {"sdx": sdx, "sdw": sdw, "sg": sg, "sv": sv, "sc": sc, "st": st})
            return nc

        @block.sync
        def _(e):
            for fn in ops["sync"]:
                fn(e)

        @block.gpsimd
        def _(e):
            for fn in ops["gpsimd"]:
                fn(e)

        @block.vector
        def _(e):
            for fn in ops["vector"]:
                fn(e)

        @block.scalar
        def _(e):
            for fn in ops["scalar"]:
                fn(e)

        @block.tensor
        def _(e):
            for fn in ops["tensor"]:
                fn(e)

    return nc


_OPS_HOOK = None
_DBG = False
_LAST_RES = None
_NC_CACHE = None


def kernel(L0_attn_x, L0_attn_wq, L0_attn_bq, L0_attn_wk, L0_attn_bk, L0_attn_wv,
           L0_attn_bv, L0_attn_wo, L0_attn_bo, L0_ffn_weight, L0_ffn_bias,
           L0_ffn_ln_gamma, L0_ffn_ln_beta):
    global _NC_CACHE
    x = np.ascontiguousarray(np.asarray(L0_attn_x, dtype=np.float32))
    shared = {
        "wq": np.ascontiguousarray(np.asarray(L0_attn_wq, np.float32)),
        "wk": np.ascontiguousarray(np.asarray(L0_attn_wk, np.float32)),
        "wv": np.ascontiguousarray(np.asarray(L0_attn_wv, np.float32)),
        "wo": np.ascontiguousarray(np.asarray(L0_attn_wo, np.float32)),
        "bq": np.ascontiguousarray(np.asarray(L0_attn_bq, np.float32)),
        "bk": np.ascontiguousarray(np.asarray(L0_attn_bk, np.float32)),
        "bv": np.ascontiguousarray(np.asarray(L0_attn_bv, np.float32)),
        "bo": np.ascontiguousarray(np.asarray(L0_attn_bo, np.float32)),
        "fw": np.ascontiguousarray(np.asarray(L0_ffn_weight, np.float32)),
        "fb": np.ascontiguousarray(np.asarray(L0_ffn_bias, np.float32)),
        "gam": np.ascontiguousarray(np.asarray(L0_ffn_ln_gamma, np.float32)),
        "bet": np.ascontiguousarray(np.asarray(L0_ffn_ln_beta, np.float32)),
    }
    if _NC_CACHE is None:
        _NC_CACHE = build_nc()
    nc = _NC_CACHE
    in_maps = []
    for c in range(8):
        b = c // 4
        q0 = SL * (c % 4)
        im = dict(shared)
        im["x_full"] = np.ascontiguousarray(x[b])
        im["x_loc"] = np.ascontiguousarray(x[b, q0:q0 + SL])
        im["pos_base"] = np.array([[float(q0)]], dtype=np.float32)
        in_maps.append(im)
    res = run_bass_kernel_spmd(nc, in_maps, core_ids=list(range(8)))
    global _LAST_RES
    _LAST_RES = res
    outa = np.empty((2, S, D), np.float32)
    for c in range(8):
        b = c // 4
        q0 = SL * (c % 4)
        outa[b, q0:q0 + SL] = res.results[c]["out"]
    return outa


if __name__ == "__main__":
    nc = build_nc()
    print("built ok; sbuf remaining:", nc.sbuf_bytes_remaining)


# version 41 (final)
# speedup vs baseline: 1.0091x; 1.0091x over previous
import numpy as np
import concourse.bass as bass
import concourse.mybir as mybir
from concourse.bass_utils import run_bass_kernel_spmd

F32 = mybir.dt.float32
I32 = mybir.dt.int32
BF16 = mybir.dt.bfloat16
AX = mybir.AxisListType.X
OP = mybir.AluOpType
ACT = mybir.ActivationFunctionType

D = 2048
NDC = 16
S = 4096
SL = 1024
H = 16
HD = 128
G = 2
KVF = 256
NT_F = 32
NT_L = 8
NGRP = 8
EPS = 1e-5
QEPS = 1e-5
TWO_PI = float(2.0 * np.pi)
PI_F = float(np.float32(np.pi))
PI_LO = float(np.nextafter(np.float32(np.pi), np.float32(0.0)))
INV_2PI = float(np.float32(1.0 / (2.0 * np.pi)))
LN1E4 = float(np.log(10000.0))

B_XLNT_LOC = 0
B_ATTNT = 0
B_XN2T = 0
B_XLNT_GRP = 16384
B_WCOL = 16384
B_WQT = 16384
B_WQROW = 20480
B_KT = 24576
B_FWB = 24576
B_VTOK = 32768
B_QSCR = 32768
B_QT = 40960
B_Y = 40960
B_XLN = 57344
B_VTSCR = 59392
B_PROBS = 60416
B_SQSCR = 60416
B_WK = 62464
B_WV = 66560
B_BETBF = 70656
B_ONES = 70672
B_ONESBF = B_WK + 2048
B_IDENT = 70688
NB = 70848

F_X1 = 0
F_XB = 0
F_QCOS = 4096
F_QSIN = 5120
F_KCOS = 6144
F_KSIN = 7168
F_TG1 = 8192
F_TG2 = 9216
F_TGI = 10240
F_RTMP = 11264
F_RU = 12288
F_ROWSTG = 13312
F_ACCV = 12288
F_ACCG = 13312
F_RECIPB = 14336
F_ONESROW = 15360
F_POSROW = 15488
F_THROW = 15490
F_OSB = 16384
F_RS = 17408
F_NEGM = 17412
F_SSQ = 17416
F_VARP = 17420
F_ST = 17424
F_RSTD = 17428
F_BQ = 17440
F_BQSH = 17456
F_BO = 17472
F_FB = 17488
F_GAM = 17504
F_CB = 17520
F_BK = 17536
F_BKSH = 17538
F_BV = 17540
F_WABS = 17544
F_WABS1 = 17560
F_THR = 17561
F_NTHR = 17562
F_SVQ = 17563
F_SVK = 17564
F_POSB = 17565
F_INVF = 17566
F_ONESP = 17567
F_SPACER = 17568
F_CEPS = 17569
F_IDENTF = 17600
NF = 17728


class Sem:
    def __init__(self, h):
        self.h = h
        self.n = 0

    def inc(self, k=1):
        self.n += k
        return self.n


def build_nc():
    nc = bass.Bass("TRN2", target_bir_lowering=False)

    x_full = nc.dram_tensor("x_full", [S, D], F32, kind="ExternalInput")
    x_loc = nc.dram_tensor("x_loc", [SL, D], F32, kind="ExternalInput")
    pos_base = nc.dram_tensor("pos_base", [1, 1], F32, kind="ExternalInput")
    wq = nc.dram_tensor("wq", [D, D], F32, kind="ExternalInput")
    wk = nc.dram_tensor("wk", [D, KVF], F32, kind="ExternalInput")
    wv = nc.dram_tensor("wv", [D, KVF], F32, kind="ExternalInput")
    wo = nc.dram_tensor("wo", [D, D], F32, kind="ExternalInput")
    bq = nc.dram_tensor("bq", [D], F32, kind="ExternalInput")
    bk = nc.dram_tensor("bk", [KVF], F32, kind="ExternalInput")
    bv = nc.dram_tensor("bv", [KVF], F32, kind="ExternalInput")
    bo = nc.dram_tensor("bo", [D], F32, kind="ExternalInput")
    fw = nc.dram_tensor("fw", [D, D], F32, kind="ExternalInput")
    fb = nc.dram_tensor("fb", [D], F32, kind="ExternalInput")
    gam = nc.dram_tensor("gam", [D], F32, kind="ExternalInput")
    bet = nc.dram_tensor("bet", [D], F32, kind="ExternalInput")
    out = nc.dram_tensor("out", [SL, D], F32, kind="ExternalOutput")
    dbg = nc.dram_tensor("dbg", [4096, 2048], F32, kind="ExternalOutput") if _DBG else None

    ab = nc.alloc_sbuf_tensor("ab", [128, NB], BF16)
    af = nc.alloc_sbuf_tensor("af", [128, NF], F32)

    def A(off, w, p0=0, pn=128):
        return bass.AP(ab, p0 * NB + off, [[NB, pn], [1, w]])

    def A3(off, step, cnt, w, p0=0, pn=128):
        return bass.AP(ab, p0 * NB + off, [[NB, pn], [step, cnt], [1, w]])

    def F(off, w, p0=0, pn=128):
        return bass.AP(af, p0 * NF + off, [[NF, pn], [1, w]])

    def F3(off, step, cnt, w, p0=0, pn=128):
        return bass.AP(af, p0 * NF + off, [[NF, pn], [step, cnt], [1, w]])

    def FI(off, w):
        return bass.AP(af, off, [[NF, 128], [1, w]]).bitcast(I32)

    ops = {k: [] for k in ("tensor", "vector", "scalar", "gpsimd", "sync")}

    def em(eng, fn):
        ops[eng].append(fn)

    def wait(eng, sem, val):
        if val > 0:
            em(eng, lambda e, _s=sem.h, _v=val: e.wait_ge(_s, _v))

    with (
        nc.psum_tensor("pA", [128, 1024], F32) as pA,
        nc.psum_tensor("pB", [128, 1024], F32) as pB,
        nc.psum_tensor("pC", [128, 1024], F32) as pC,
        nc.psum_tensor("pD", [128, 1024], F32) as pD,
        nc.semaphore("sdx") as _sdx,
        nc.semaphore("sdw") as _sdw,
        nc.semaphore("sg") as _sg,
        nc.semaphore("sv") as _sv,
        nc.semaphore("sc") as _sc,
        nc.semaphore("st") as _st,
        nc.Block() as block,
    ):
        sdx, sdw, sg, sv, sc, st = (Sem(x) for x in (_sdx, _sdw, _sg, _sv, _sc, _st))
        sdt = [Sem(nc.alloc_semaphore(name=f"sdt{i}")) for i in range(4)]
        sdxl, sdo, sdp, sdwk, sdbr = (Sem(nc.alloc_semaphore(name=n)) for n in
                                      ("sdxl", "sdo", "sdp", "sdwk", "sdbr"))
        sdf = [Sem(nc.alloc_semaphore(name=f"sdf{i}")) for i in range(4)]
        sdq = [Sem(nc.alloc_semaphore(name=f"sdq{i}")) for i in range(2)]
        sdqa = [Sem(nc.alloc_semaphore(name=f"sdqa{i}")) for i in range(4)]
        RECIP_ROW = lambda w=1024, o=0: bass.AP(ab, B_XLN + 2 * o, [[NB, 1], [1, 2 * w]]).bitcast(F32)

        B_PRPAIR = B_VTSCR
        B_PRPAIR2 = B_WQROW + 2048

        def PROBS_OFF(k):
            b = k % 8
            if b < 2:
                return B_PROBS + b * 1024
            if b < 4:
                return B_WQROW + (b - 2) * 1024
            if b < 6:
                return B_WK + (b - 4) * 1024
            return B_WV + (b - 6) * 1024
        psum_half = {
            ("pA", 0): pA[:, 0:512], ("pA", 1): pA[:, 512:1024],
            ("pB", 0): pB[:, 0:512], ("pB", 1): pB[:, 512:1024],
            ("pC", 0): pC[:, 0:512], ("pC", 1): pC[:, 512:1024],
        }

        em("gpsimd", lambda e: e.iota(F(F_TG1, 128), [[0, 128]], channel_multiplier=1,
                                      allow_small_or_imprecise_dtypes=True))
        em("gpsimd", lambda e: e.iota(F(F_TG1 + 128, 128), [[1, 128]], channel_multiplier=0,
                                      allow_small_or_imprecise_dtypes=True))
        em("gpsimd", lambda e: e.iota(F(F_INVF, 1, 0, 64), [[0, 1]], channel_multiplier=1,
                                      allow_small_or_imprecise_dtypes=True).then_inc(sg.h, 1))
        g_iv = sg.inc()
        wait("gpsimd", sg, g_iv)
        em("gpsimd", lambda e: e.tensor_copy(F(F_INVF, 1, 64, 64), F(F_INVF, 1, 0, 64)))
        em("gpsimd", lambda e: e.iota(F(F_TG2, 1024), [[1, 1024]], channel_multiplier=0,
                                      allow_small_or_imprecise_dtypes=True).then_inc(sg.h, 1))
        g_setup = sg.inc()

        em("gpsimd", lambda e: e.dma_start(out=F(F_POSROW, 1, 0, 1),
                                           in_=pos_base.ap()[:, :]).then_inc(sdp.h, 16))
        w_pos = sdp.inc(16)
        for half in range(2):
            em("gpsimd", lambda e, _h=half: e.dma_start(
                out=A3(B_WK + _h * 8 * 256, 256, 8, 256),
                in_=bass.AP(wk, _h * 8 * 128 * KVF, [[KVF, 128], [128 * KVF, 8], [1, 256]]))
                .then_inc(sdwk.h, 16))
            sdwk.inc(16)
            em("gpsimd", lambda e, _h=half: e.dma_start(
                out=A3(B_WV + _h * 8 * 256, 256, 8, 256),
                in_=bass.AP(wv, _h * 8 * 128 * KVF, [[KVF, 128], [128 * KVF, 8], [1, 256]]))
                .then_inc(sdwk.h, 16))
            sdwk.inc(16)
        em("gpsimd", lambda e: e.dma_start(out=F(F_BK, 2), in_=bass.AP(bk, 0, [[1, 128], [128, 2]]),
                                           allow_slow_non_contiguous=True).then_inc(sdwk.h, 16))
        sdwk.inc(16)
        em("gpsimd", lambda e: e.dma_start(out=F(F_BV, 2), in_=bass.AP(bv, 0, [[1, 128], [128, 2]]),
                                           allow_slow_non_contiguous=True).then_inc(sdwk.h, 16))
        w_small = sdwk.inc(16)

        wait("vector", sg, g_setup)
        em("vector", lambda e: e.tensor_tensor(F(F_IDENTF, 128), F(F_TG1, 128), F(F_TG1 + 128, 128),
                                               OP.is_equal).then_inc(sv.h, 1))
        _m = sv.inc()
        wait("vector", sv, _m)
        em("vector", lambda e: e.tensor_copy(A(B_IDENT, 128), F(F_IDENTF, 128)))
        em("vector", lambda e: e.memset(A(B_ONES, 1), 1.0))
        em("vector", lambda e: e.memset(F(F_ONESP, 1), 1.0))
        em("vector", lambda e: e.memset(F(F_ONESROW, 128), 1.0))
        em("vector", lambda e: e.memset(F(F_SVQ, 1, 0, 64), -1.0 / 128.0))
        em("vector", lambda e: e.memset(F(F_SVQ, 1, 64, 64), 1.0 / 128.0))
        em("vector", lambda e: e.memset(F(F_CEPS, 1), EPS))
        em("vector", lambda e: e.memset(F(F_SVK, 1, 0, 64), -1.0))
        em("vector", lambda e: e.memset(F(F_SVK, 1, 64, 64), 1.0))
        em("vector", lambda e: e.tensor_scalar(F(F_INVF, 1), F(F_INVF, 1), -LN1E4 / 64.0, None,
                                               OP.mult).then_inc(sv.h, 1))
        v_setup = sv.inc()
        wait("scalar", sv, v_setup)
        em("scalar", lambda e: e.activation(F(F_INVF, 1), F(F_INVF, 1), ACT.Exp).then_inc(sc.h, 1))
        c_invf = sc.inc()

        wait("tensor", sdp, w_pos)
        wait("tensor", sv, v_setup)
        em("tensor", lambda e: e.matmul(pC[:, 0:1], F(F_ONESROW, 128, 0, 1), F(F_POSROW, 1, 0, 1),
                                        start=True, stop=True).then_inc(st.h, 1))
        t_posb = st.inc()
        wait("vector", st, t_posb)
        em("vector", lambda e: e.tensor_copy(F(F_POSB, 1), pC[:, 0:1]).then_inc(sv.h, 1))
        v_posb = sv.inc()

        bias_rows = [(bq, F_BQ, "f"), (bo, F_BO, "f"), (fb, F_FB, "f"), (gam, F_GAM, "f"),
                     (bet, B_BETBF, "b")]
        prev_consume = 0
        bias_done_marks = []
        for i, (t, dst, dt) in enumerate(bias_rows):
            wait("gpsimd", st, prev_consume)
            em("gpsimd", lambda e, _t=t: e.dma_start(out=F(F_ROWSTG, 2048, 0, 1),
                                                     in_=bass.AP(_t, 0, [[2048, 1], [1, 2048]]))
               .then_inc(sdbr.h, 16))
            dmark = sdbr.inc(16)
            wait("tensor", sdbr, dmark)
            if bias_done_marks:
                wait("tensor", sv, bias_done_marks[-1])
            for c in range(NDC):
                last = c == NDC - 1
                def mk(e, _c=c, _last=last):
                    ins = e.matmul(pC[:, 16 + _c:17 + _c], F(F_ROWSTG + _c * 128, 128, 0, 1),
                                   F(F_ONESP, 1, 0, 1), start=True, stop=True)
                    if _last:
                        ins.then_inc(st.h, 1)
                em("tensor", mk)
            prev_consume = st.inc()
            wait("vector", st, prev_consume)
            if dt == "f":
                em("vector", lambda e, _d=dst: e.tensor_copy(F(_d, 16), pC[:, 16:32]).then_inc(sv.h, 1))
            else:
                em("vector", lambda e, _d=dst: e.tensor_copy(A(_d, 16), pC[:, 16:32]).then_inc(sv.h, 1))
            bias_done_marks.append(sv.inc())
        t_bias_done = prev_consume

        wait("vector", sdwk, w_small)
        wait("vector", sv, bias_done_marks[0])
        em("vector", lambda e: e.tensor_copy(F(F_BQSH, 16, 0, 64), F(F_BQ, 16, 64, 64)))
        em("vector", lambda e: e.tensor_copy(F(F_BQSH, 16, 64, 64), F(F_BQ, 16, 0, 64)))
        em("vector", lambda e: e.tensor_copy(F(F_BKSH, 2, 0, 64), F(F_BK, 2, 64, 64)))
        em("vector", lambda e: e.tensor_copy(F(F_BKSH, 2, 64, 64), F(F_BK, 2, 0, 64)).then_inc(sv.h, 1))
        v_bsh = sv.inc()

        def vstep(fn):
            em("vector", lambda e, _f=fn: _f(e).then_inc(sv.h, 1))
            m = sv.inc()
            wait("vector", sv, m)
            return m

        def range_reduce(dst_off, src_off, width):
            vstep(lambda e: e.tensor_scalar(F(F_RTMP, width), F(src_off, width), INV_2PI,
                                            None, OP.mult))
            vstep(lambda e: e.tensor_copy(FI(F_TGI, width), F(F_RTMP, width)))
            vstep(lambda e: e.tensor_copy(F(F_RTMP, width), FI(F_TGI, width)))
            vstep(lambda e: e.scalar_tensor_tensor(F(dst_off, width), F(F_RTMP, width),
                                                   -TWO_PI, F(src_off, width), OP.mult, OP.add))
            vstep(lambda e: e.tensor_scalar(F(F_RTMP, width), F(dst_off, width), PI_F,
                                            None, OP.is_gt))
            vstep(lambda e: e.scalar_tensor_tensor(F(dst_off, width), F(F_RTMP, width),
                                                   -TWO_PI, F(dst_off, width), OP.mult, OP.add))
            vstep(lambda e: e.tensor_scalar(F(dst_off, width), F(dst_off, width), PI_LO,
                                            -PI_LO, OP.min, OP.max))

        wait("vector", sv, v_posb)
        wait("vector", sc, c_invf)
        vstep(lambda e: e.tensor_scalar(F(F_TG2, 1024), F(F_TG2, 1024), F(F_POSB, 1),
                                        None, OP.add))
        vstep(lambda e: e.tensor_scalar(F(F_TG1, 1024), F(F_TG2, 1024), F(F_INVF, 1),
                                        None, OP.mult))
        range_reduce(F_QSIN, F_TG1, 1024)
        vstep(lambda e: e.tensor_scalar(F(F_TG2, 1024), F(F_TG1, 1024), PI_F / 2.0,
                                        None, OP.add))
        range_reduce(F_QCOS, F_TG2, 1024)
        em("vector", lambda e: e.tensor_copy(F(F_SPACER, 1), F(F_ONESP, 1)).then_inc(sv.h, 1))
        v_qred = sv.inc()
        wait("scalar", sv, v_qred)
        em("scalar", lambda e: e.activation(F(F_QSIN, 1024), F(F_QSIN, 1024), ACT.Sin))
        em("scalar", lambda e: e.activation(F(F_QCOS, 1024), F(F_QCOS, 1024), ACT.Sin)
           .then_inc(sc.h, 1))
        c_qtab = sc.inc()
        wait("vector", sc, c_qtab)
        em("vector", lambda e: e.tensor_scalar(F(F_QSIN, 1024), F(F_QSIN, 1024), F(F_SVQ, 1),
                                               None, OP.mult))
        em("vector", lambda e: e.tensor_scalar(F(F_QCOS, 1024), F(F_QCOS, 1024), 1.0 / 128.0,
                                               None, OP.mult).then_inc(sv.h, 1))
        v_qtab = sv.inc()

        ktab_seg_marks = []
        prev_segsin = 0
        for seg in range(4):
            if prev_segsin:
                wait("gpsimd", sc, prev_segsin)
            else:
                wait("gpsimd", sv, v_qtab)
            em("gpsimd", lambda e, _s=seg: e.iota(F(F_TG2, 1024), [[1, 1024]], base=_s * 1024,
                                                  channel_multiplier=0,
                                                  allow_small_or_imprecise_dtypes=True)
               .then_inc(sg.h, 1))
            smark = sg.inc()
            wait("vector", sg, smark)
            vstep(lambda e: e.tensor_scalar(F(F_TG1, 1024), F(F_TG2, 1024), F(F_INVF, 1),
                                            None, OP.mult))
            range_reduce(F_TG2, F_TG1, 1024)
            vstep(lambda e: e.tensor_scalar(F(F_TG1, 1024), F(F_TG1, 1024), PI_F / 2.0,
                                            None, OP.add))
            range_reduce(F_RU, F_TG1, 1024)
            em("vector", lambda e: e.tensor_copy(F(F_SPACER, 1), F(F_ONESP, 1)).then_inc(sv.h, 1))
            segred = sv.inc()
            wait("scalar", sv, segred)
            em("scalar", lambda e, _s=seg: e.activation(A(B_KT + 4096 + _s * 1024, 1024),
                                                        F(F_TG2, 1024), ACT.Sin))
            em("scalar", lambda e, _s=seg: e.activation(A(B_KT + _s * 1024, 1024),
                                                        F(F_RU, 1024), ACT.Sin).then_inc(sc.h, 1))
            segsin = sc.inc()
            prev_segsin = segsin
            wait("vector", sc, segsin)
            em("vector", lambda e, _s=seg: e.tensor_scalar(
                A(B_KT + 4096 + _s * 1024, 1024), A(B_KT + 4096 + _s * 1024, 1024),
                F(F_SVK, 1), None, OP.mult).then_inc(sv.h, 1))
            ktab_seg_marks.append(sv.inc())
        v_ktabs = sv.n

        apply_marks = {}

        LASTSQ = [0]

        def ln_tile(key, src_fn, jr, out_fn, apply_gate=0, src_gate=0):
            em("vector", lambda e: e.memset(F(F_SSQ + jr, 1), 0.0))
            if src_gate:
                wait("vector", sv, src_gate)
            em("vector", lambda e, f=src_fn: e.reduce_sum(F(F_RS + jr, 1), f(), axis=AX)
               .then_inc(sv.h, 1))
            m1 = sv.inc()
            wait("vector", sv, m1)
            em("vector", lambda e: e.tensor_scalar(F(F_NEGM + jr, 1), F(F_RS + jr, 1), -1.0 / D,
                                                   None, OP.mult).then_inc(sv.h, 1))
            m2 = sv.inc()
            wait("scalar", sv, m2)
            wait("scalar", sc, LASTSQ[0])
            em("scalar", lambda e, f=src_fn: e.activation(A(B_SQSCR, 2048), f(), ACT.Square,
                                                          bias=F(F_NEGM + jr, 1),
                                                          accum_out=F(F_SSQ + jr, 1)).then_inc(sc.h, 1))
            m3 = sc.inc()
            LASTSQ[0] = m3
            wait("vector", sc, m3)
            em("vector", lambda e: e.tensor_scalar(F(F_VARP + jr, 1), F(F_SSQ + jr, 1), 1.0 / D,
                                                   EPS, OP.mult, OP.add).then_inc(sv.h, 1))
            m4 = sv.inc()
            wait("scalar", sv, m4)
            em("scalar", lambda e: e.activation(F(F_ST + jr, 1), F(F_VARP + jr, 1), ACT.Sqrt)
               .then_inc(sc.h, 1))
            m5 = sc.inc()
            wait("vector", sc, m5)
            em("vector", lambda e: e.reciprocal(F(F_RSTD + jr, 1), F(F_ST + jr, 1)).then_inc(sv.h, 1))
            m6 = sv.inc()
            wait("vector", sv, m6)
            if apply_gate:
                wait("vector", st, apply_gate)
            em("vector", lambda e, f=src_fn, o=out_fn: e.tensor_scalar(
                o(), f(), F(F_NEGM + jr, 1), F(F_RSTD + jr, 1), OP.add, OP.mult)
                .then_inc(sv.h, 1))
            apply_marks[key] = sv.inc()

        tp_gate = []

        def transpose16(xln_off, dest_fn):
            for r in range(4):
                half = r % 2
                if len(tp_gate) >= 2:
                    wait("tensor", sv, tp_gate[-2])
                for cth in range(4):
                    dc = 4 * r + cth
                    last = cth == 3
                    def mk(e, _dc=dc, _half=half, _cth=cth, _xo=xln_off, _last=last):
                        dstv = pA[:, _half * 512:(_half + 1) * 512].bitcast(BF16)
                        ins = e.transpose(dstv[:, _cth * 128:(_cth + 1) * 128],
                                          A(_xo + _dc * 128, 128), A(B_IDENT, 128))
                        if _last:
                            ins.then_inc(st.h, 1)
                    em("tensor", mk)
                tmark = st.inc()
                wait("vector", st, tmark)
                em("vector", lambda e, _half=half, _r=r, _dfn=dest_fn: e.tensor_copy(
                    _dfn(_r), pA[:, _half * 512:(_half + 1) * 512].bitcast(BF16)[:, 0:512])
                    .then_inc(sv.h, 1))
                tp_gate.append(sv.inc())

        tp_gate2 = []

        def p_stats(jr, src_fn, dmark):
            dsem, dval = dmark
            em("vector", lambda e: e.memset(F(F_SSQ + jr, 1), 0.0).then_inc(sv.h, 1))
            zmark = sv.inc()
            wait("vector", dsem, dval)
            em("vector", lambda e, f=src_fn: e.reduce_sum(F(F_RS + jr, 1), f(), axis=AX)
               .then_inc(sv.h, 1))
            rmark = sv.inc()
            wait("scalar", sv, zmark)
            wait("scalar", dsem, dval)
            wait("scalar", sc, LASTSQ[0])
            em("scalar", lambda e, f=src_fn: e.activation(A(B_SQSCR, 2048), f(), ACT.Square,
                                                          accum_out=F(F_SSQ + jr, 1)).then_inc(sc.h, 1))
            qmark = sc.inc()
            LASTSQ[0] = qmark
            return rmark, qmark

        def p_finish(jr, src_fn, out_fn, rmark, qmark, apply_gate_st, apply_gate_sv=0):
            wait("vector", sv, rmark)
            em("vector", lambda e: e.tensor_scalar(F(F_NEGM + jr, 1), F(F_RS + jr, 1), -1.0 / D,
                                                   None, OP.mult).then_inc(sv.h, 1))
            nmark = sv.inc()
            wait("vector", sv, nmark)
            em("vector", lambda e: e.tensor_tensor(F(F_ST + jr, 1), F(F_NEGM + jr, 1),
                                                   F(F_NEGM + jr, 1), OP.mult).then_inc(sv.h, 1))
            mmark = sv.inc()
            wait("vector", sv, mmark)
            wait("vector", sc, qmark)
            em("vector", lambda e: e.scalar_tensor_tensor(F(F_VARP + jr, 1), F(F_SSQ + jr, 1),
                                                          1.0 / D, F(F_ST + jr, 1), OP.mult,
                                                          OP.subtract).then_inc(sv.h, 1))
            vmark = sv.inc()
            wait("scalar", sv, vmark)
            em("scalar", lambda e: e.activation(F(F_ST + jr, 1), F(F_VARP + jr, 1), ACT.Sqrt,
                                                bias=F(F_CEPS, 1)).then_inc(sc.h, 1))
            smark = sc.inc()
            wait("vector", sc, smark)
            em("vector", lambda e: e.reciprocal(F(F_RSTD + jr, 1), F(F_ST + jr, 1)).then_inc(sv.h, 1))
            rcm = sv.inc()
            wait("vector", sv, rcm)
            if apply_gate_st:
                wait("vector", st, apply_gate_st)
            if apply_gate_sv:
                wait("vector", sv, apply_gate_sv)
            em("vector", lambda e, f=src_fn, o=out_fn: e.tensor_scalar(
                o(), f(), F(F_NEGM + jr, 1), F(F_RSTD + jr, 1), OP.add, OP.mult)
                .then_inc(sv.h, 1))
            return sv.inc()

        def p_transpose16(xln_off, dest_fn, amark):
            wait("tensor", sv, amark)
            for r in range(4):
                half = r % 2
                if len(tp_gate2) >= 2:
                    geng, gm = tp_gate2[-2]
                    wait("tensor", sv if geng == "v" else sc, gm)
                for cth in range(4):
                    dc = 4 * r + cth
                    last = cth == 3
                    def mk(e, _dc=dc, _half=half, _cth=cth, _xo=xln_off, _last=last):
                        dstv = pA[:, _half * 512:(_half + 1) * 512].bitcast(BF16)
                        ins = e.transpose(dstv[:, _cth * 128:(_cth + 1) * 128],
                                          A(_xo + _dc * 128, 128), A(B_IDENT, 128))
                        if _last:
                            ins.then_inc(st.h, 1)
                    em("tensor", mk)
                tmark = st.inc()
                wait("vector", st, tmark)
                em("vector", lambda e, _half=half, _r=r, _dfn=dest_fn: e.tensor_copy(
                    _dfn(_r), pA[:, _half * 512:(_half + 1) * 512].bitcast(BF16)[:, 0:512])
                    .then_inc(sv.h, 1))
                tp_gate2.append(("v", sv.inc()))

        def XB_AP(i):
            b = i % 3
            if b < 2:
                return F(F_XB + b * 2048, 2048)
            return F(F_ACCG, 2048)

        def XB4(i):
            b = i % 4
            if b < 2:
                return F(F_XB + b * 2048, 2048)
            if b == 2:
                return F(F_ACCG, 2048)
            return F(F_QCOS, 2048)

        last_tp_st = 0
        pendq = []

        def flush_one_local():
            _j, _jr, _sf, _rm, _qm = pendq.pop(0)
            global_unused = None
            am = p_finish(_jr, _sf, (lambda: A(B_XLN, 2048)), _rm, _qm, last_tp_st)
            apply_marks[("L", _j)] = am
            dest_fn = (lambda __j=_j: (lambda r: A3(B_XLNT_LOC + (4 * r) * 1024 + __j * 128,
                                                    1024, 4, 128)))()
            p_transpose16(B_XLN, dest_fn, am)
            return st.n

        for j in range(NT_L):
            jr = j % 4
            if j == 2:
                wait("sync", st, t_bias_done)
            if j >= 3:
                wait("sync", sv, apply_marks[("L", j - 3)])
            _S = sdt[j % 3]
            em("sync", lambda e, _j=j, _a=XB_AP(j), _s=_S.h: e.dma_start(
                out=_a, in_=bass.AP(x_loc, _j * 128 * D, [[D, 128], [1, D]])).then_inc(_s, 16))
            dmark = (_S, _S.inc(16))
            src_fn = (lambda _j=j: (lambda: XB_AP(_j)))()
            rmark, qmark = p_stats(jr, src_fn, dmark)
            pendq.append((j, jr, src_fn, rmark, qmark))
            if len(pendq) > 2:
                last_tp_st = flush_one_local()
        while pendq:
            last_tp_st = flush_one_local()
        geng, gm = tp_gate2[-1]
        v_xlnt_loc_e = (geng, gm)
        geng2, gm2 = tp_gate2[-2]
        v_xlnt_loc = gm if geng == "v" else gm2


        qps = {0: pA, 1: pB}
        rope_q_marks = []
        qmm_marks = []
        for oc in range(NDC):
            if oc >= 4:
                wait("gpsimd", st, qmm_marks[oc - 4])
            for half in range(2):
                em("gpsimd", lambda e, _oc=oc, _h=half, _s=sdqa[oc % 4].h: e.dma_start(
                    out=A3(B_WCOL + (_oc % 4) * 2048 + _h * 8 * 128, 128, 8, 128),
                    in_=bass.AP(wq, (_h * 8 * 128) * D + _oc * 128,
                                [[D, 128], [128 * D, 8], [1, 128]])).then_inc(_s, 16))
                sdqa[oc % 4].inc(16)
            qdma_mark = sdqa[oc % 4].n
            P = qps[oc % 2]
            wait("tensor", sdqa[oc % 4], qdma_mark)
            if oc == 0:
                last_s_evac = max((m for g_, m in tp_gate2 if g_ == "s"), default=0)
                last_v_evac = max((m for g_, m in tp_gate2 if g_ == "v"), default=0)
                wait("tensor", sc, last_s_evac)
                wait("tensor", sv, last_v_evac)
                wait("tensor", sv, bias_done_marks[-1])
            if oc >= 2:
                wait("tensor", sv, rope_q_marks[oc - 2])
            for dc in range(NDC):
                for half in range(2):
                    last = dc == NDC - 1 and half == 1
                    def mk(e, _P=P, _oc=oc, _dc=dc, _h=half, _last=last):
                        ins = e.matmul(_P[:, _h * 512:(_h + 1) * 512],
                                       A(B_WCOL + (_oc % 4) * 2048 + _dc * 128, 128),
                                       A(B_XLNT_LOC + _dc * 1024 + _h * 512, 512),
                                       start=(_dc == 0), stop=(_dc == NDC - 1))
                        if _last:
                            ins.then_inc(st.h, 1)
                    em("tensor", mk)
            qmm_marks.append(st.inc())
            wait("vector", st, qmm_marks[oc])
            if oc == 0:
                wait("vector", sv, v_qtab)
                wait("vector", sv, v_bsh)
            for half in range(2):
                hw = 512
                base = half * 512
                em("vector", lambda e, _P=P, _oc=oc, _b=base: e.scalar_tensor_tensor(
                    F(F_RTMP + _b, 512), _P[:, _b:_b + 512], F(F_BQ + _oc, 1), F(F_QCOS + _b, 512),
                    OP.add, OP.mult))
                em("vector", lambda e, _P=P, _oc=oc, _b=base: e.scalar_tensor_tensor(
                    F(F_RU + _b, 512, 0, 64), _P[64:128, _b:_b + 512], F(F_BQSH + _oc, 1, 0, 64),
                    F(F_QSIN + _b, 512, 0, 64), OP.add, OP.mult))
                em("vector", lambda e, _P=P, _oc=oc, _b=base: e.scalar_tensor_tensor(
                    F(F_RU + _b, 512, 64, 64), _P[0:64, _b:_b + 512], F(F_BQSH + _oc, 1, 64, 64),
                    F(F_QSIN + _b, 512, 64, 64), OP.add, OP.mult).then_inc(sv.h, 1))
                _mq = sv.inc()
                wait("vector", sv, _mq)
                em("vector", lambda e, _oc=oc, _b=base: e.tensor_add(
                    A(B_QT + _oc * 1024 + _b, 512), F(F_RTMP + _b, 512), F(F_RU + _b, 512))
                    .then_inc(sv.h, 1))
                sv.inc()
            rope_q_marks.append(sv.n)

        tp_gate2.append(("v", rope_q_marks[-1]))
        tp_gate2.append(("v", rope_q_marks[-1]))
        kv_rope_marks = []
        vtok_marks = []
        vt_gate = []
        kv_pend = []
        kvmm_marks = {}
        XG = lambda _g: B_XLNT_GRP if _g % 2 == 0 else (B_XLNT_LOC + 8192)

        def queue_kv(g):
            evac_ready = tp_gate2[-1][1]
            for ch in range(4):
                def emit_chunk(_g=g, _ch=ch, _ev=evac_ready):
                    if _ch == 0:
                        wait("tensor", sv, _ev)
                        if _g == 0:
                            wait("tensor", sv, rope_q_marks[-1])
                            wait("tensor", sdwk, w_small)
                        else:
                            wait("tensor", sv, kv_rope_marks[-1])
                    for dc in range(_ch * 4, _ch * 4 + 4):
                        last = dc == NDC - 1
                        for c in range(2):
                            def mkk(e, _dc=dc, _c=c, _gg=_g):
                                e.matmul(pB[:, _c * 512:(_c + 1) * 512],
                                         A(B_WK + _dc * 256 + _c * 128, 128),
                                         A(XG(_gg) + _dc * 512, 512),
                                         start=(_dc == 0), stop=(_dc == NDC - 1))
                            em("tensor", mkk)
                        for c in range(2):
                            def mkv(e, _dc=dc, _c=c, _gg=_g, _last=last):
                                ins = e.matmul(pC[:, _c * 512:(_c + 1) * 512],
                                               A(B_WV + _dc * 256 + _c * 128, 128),
                                               A(XG(_gg) + _dc * 512, 512),
                                               start=(_dc == 0), stop=(_dc == NDC - 1))
                                if _last and _c == 1:
                                    ins.then_inc(st.h, 1)
                            em("tensor", mkv)
                    if _ch == 3:
                        kvmm_marks[_g] = st.inc()
                kv_pend.append(emit_chunk)

        def flush_pend():
            _i, _jr, _sf, _rm, _qm = pend.pop(0)
            _g = _i // 4
            if _i % 4 == 0 and _g >= 2:
                wait("vector", st, kvmm_marks[_g - 2])
            xoff = B_XLN if _i % 2 == 0 else B_XLNT_LOC
            agate = last_tp2[_i % 2]
            if _i % 2 == 1 and not xln2_init[0]:
                agate = 0
                wait("vector", st, qmm_marks[-1])
                xln2_init[0] = True
            am = p_finish(_jr, _sf, (lambda _x=xoff: (lambda: A(_x, 2048)))(), _rm, _qm, agate)
            apply_marks[("F", _i)] = am
            _tt = _i % 4
            dest_fn = (lambda __tt=_tt, __g=_g: (lambda r: A3(
                XG(__g) + (4 * r) * 512 + __tt * 128, 512, 4, 128)))()
            p_transpose16(xoff, dest_fn, am)
            last_tp2[_i % 2] = st.n

        def rope_vtrans(g):
            wait("vector", st, kvmm_marks[g])
            if g == 0:
                wait("vector", sv, v_ktabs)
            for c in range(2):
                em("vector", lambda e, _c=c, _g=g: e.scalar_tensor_tensor(
                    F(F_RTMP + _c * 512, 512), pB[:, _c * 512:(_c + 1) * 512], F(F_BK + _c, 1),
                    A(B_KT + _g * 512, 512), OP.add, OP.mult))
                em("vector", lambda e, _c=c, _g=g: e.scalar_tensor_tensor(
                    F(F_RU + _c * 512, 512, 0, 64), pB[64:128, _c * 512:(_c + 1) * 512],
                    F(F_BKSH + _c, 1, 0, 64),
                    bass.AP(ab, B_KT + 4096 + _g * 512, [[NB, 64], [1, 512]]), OP.add, OP.mult))
                def mkru(e, _c=c, _g=g):
                    ins = e.scalar_tensor_tensor(
                        F(F_RU + _c * 512, 512, 64, 64), pB[0:64, _c * 512:(_c + 1) * 512],
                        F(F_BKSH + _c, 1, 64, 64),
                        bass.AP(ab, 64 * NB + B_KT + 4096 + _g * 512, [[NB, 64], [1, 512]]),
                        OP.add, OP.mult)
                    if _c == 1:
                        ins.then_inc(sv.h, 1)
                em("vector", mkru)
            src_mark = sv.inc()
            wait("vector", sv, src_mark)
            for c in range(2):
                em("vector", lambda e, _c=c, _g=g: e.tensor_add(
                    A(B_KT + _c * 4096 + _g * 512, 512), F(F_RTMP + _c * 512, 512),
                    F(F_RU + _c * 512, 512)).then_inc(sv.h, 1))
                sv.inc()
                em("vector", lambda e, _c=c: e.tensor_scalar(
                    A(B_VTSCR + _c * 512, 512), pC[:, _c * 512:(_c + 1) * 512], F(F_BV + _c, 1),
                    None, OP.add).then_inc(sv.h, 1))
                sv.inc()
            kv_rope_marks.append(sv.n)

            wait("tensor", sv, kv_rope_marks[-1])
            for r in range(2):
                half = r % 2
                if len(vt_gate) >= 2:
                    wait("tensor", sv, vt_gate[-2])
                for q4 in range(4):
                    sidx = r * 2 + q4 // 2
                    c = q4 % 2
                    last = q4 == 3
                    def mkt(e, _s=sidx, _c=c, _half=half, _last=last):
                        dstv = pD[:, _half * 512:(_half + 1) * 512].bitcast(BF16)
                        ins = e.transpose(dstv[:, (2 * (_s % 2) + _c) * 128:(2 * (_s % 2) + _c) * 128 + 128],
                                          A(B_VTSCR + _c * 512 + _s * 128, 128), A(B_IDENT, 128))
                        if _last:
                            ins.then_inc(st.h, 1)
                    em("tensor", mkt)
                tmark = st.inc()
                wait("vector", st, tmark)
                em("vector", lambda e, _g=g, _r=r, _half=half: e.tensor_copy(
                    A(B_VTOK + (_g * 4 + _r * 2) * 256, 512),
                    pD[:, _half * 512:(_half + 1) * 512].bitcast(BF16)[:, 0:512]).then_inc(sv.h, 1))
                vt_gate.append(sv.inc())
            vtok_marks.append(vt_gate[-1])

        pend = []
        last_tp2 = [last_tp_st, 0]
        xln2_init = [False]
        for g in range(NGRP):
            for tt in range(4):
                i = g * 4 + tt
                jr = i % 4
                if i >= 4:
                    gate = apply_marks.get(("F", i - 4))
                elif i == 3:
                    gate = None
                    wait("sync", sv, rope_q_marks[-1])
                else:
                    gate = apply_marks.get(("L", {0: 6, 1: 7, 2: 5}[i]))
                if gate:
                    wait("sync", sv, gate)
                _S = sdt[i % 4]
                em("sync", lambda e, _i=i, _a=XB4(i), _s=_S.h: e.dma_start(
                    out=_a, in_=bass.AP(x_full, _i * 128 * D, [[D, 128], [1, D]])).then_inc(_s, 16))
                dmark = (_S, _S.inc(16))
                src_fn = (lambda _i=i: (lambda: XB4(_i)))()
                rmark, qmark = p_stats(jr, src_fn, dmark)
                pend.append((i, jr, src_fn, rmark, qmark))
                if len(pend) > 2:
                    flush_pend()
                    if kv_pend:
                        kv_pend.pop(0)()
            while pend:
                flush_pend()
                if kv_pend:
                    kv_pend.pop(0)()
            while kv_pend:
                kv_pend.pop(0)()
            if g >= 1:
                rope_vtrans(g - 1)
            queue_kv(g)
        while kv_pend:
            kv_pend.pop(0)()
        rope_vtrans(NGRP - 1)

        v_A_done = sv.n
        kvmm_last = kvmm_marks[NGRP - 1]
        if _DBG:
            wait("sync", sv, v_A_done)
            em("sync", lambda e: e.dma_start(
                out=bass.AP(dbg, 0, [[8192, 128], [1, 8192]]),
                in_=bass.AP(ab, B_QT, [[NB, 128], [1, 16384]]).bitcast(F32)).then_inc(sdo.h, 16))
            sdo.inc(16)
            em("sync", lambda e: e.dma_start(
                out=bass.AP(dbg, 1048576, [[4096, 128], [1, 4096]]),
                in_=bass.AP(ab, B_KT, [[NB, 128], [1, 8192]]).bitcast(F32)).then_inc(sdo.h, 16))
            sdo.inc(16)
            em("sync", lambda e: e.dma_start(
                out=bass.AP(dbg, 1572864, [[4096, 128], [1, 4096]]),
                in_=bass.AP(ab, B_VTOK, [[NB, 128], [1, 8192]]).bitcast(F32)).then_inc(sdo.h, 16))
            sdo.inc(16)

        wait("sync", sv, v_A_done)
        xloc_marks = {}
        for t in range(6):
            em("sync", lambda e, _t=t: e.dma_start(
                out=F(F_X1 + _t * 2048, 2048),
                in_=bass.AP(x_loc, _t * 128 * D, [[D, 128], [1, D]])).then_inc(sdxl.h, 16))
            xloc_marks[t] = sdxl.inc(16)

        exp_marks = []
        pvsum_marks = []
        score_marks = []
        recip_marks = []
        norm_marks = []
        NKC = 32
        NKT = H * NKC

        SSETS = None

        def emit_scores(j):
            hj, kcj = divmod(j, NKC)
            P = SSETS[j % 3]
            if j >= 3:
                wait("tensor", sc, exp_marks[j - 3])
            if j == 0:
                wait("tensor", sv, tp_gate2[-1][1])
            elif j == 2:
                wait("tensor", sv, vt_gate[-1])
            for half in range(2):
                last = half == 1
                def mks(e, _P=P, _h=hj, _g=hj // 8, _kc=kcj, _half=half, _last=last):
                    ins = e.matmul(_P[:, _half * 512:(_half + 1) * 512],
                                   A(B_KT + _g * 4096 + _kc * 128, 128),
                                   A(B_QT + _h * 1024 + _half * 512, 512),
                                   start=True, stop=True)
                    if _last:
                        ins.then_inc(st.h, 1)
                em("tensor", mks)
            score_marks.append(st.inc())

        acc_eng = {kc: "vector" for kc in range(NKC)}
        acc_marks = {}
        summ_marks = []
        denb_marks = []
        denb_half_marks = []
        PCEV = bass.AP(ab, B_XLN, [[NB, 128], [1, 2048]]).bitcast(F32)
        SSETS = [pA, pB, pD]
        wait("vector", st, kvmm_last)
        em("vector", lambda e: e.memset(A(B_ONESBF, 128), 1.0).then_inc(sv.h, 1))
        ones_bf_mark = sv.inc()
        emit_scores(0)
        next_sc = [1]
        epi_pend = []
        for h in range(H):
            g = h // 8
            for kc in range(NKC):
                k = h * NKC + kc
                P = SSETS[k % 3]
                while epi_pend and epi_pend[0][0] == kc:
                    epi_pend.pop(0)[1]()
                wait("scalar", st, score_marks[k])
                if k >= 8:
                    kind, mv = acc_marks[k - 8]
                    if kind == "v":
                        wait("scalar", sv, mv)
                    elif kind == "t":
                        wait("scalar", st, mv)
                    else:
                        wait("scalar", sg, mv)
                em("scalar", lambda e, _P=P, _o=PROBS_OFF(k): e.activation(
                    A(_o, 1024), _P[:, 0:1024], ACT.Exp).then_inc(sc.h, 1))
                exp_marks.append(sc.inc())
                tgt = min(k + 3 if kc == NKC - 1 else k + 1, NKT - 1)
                while next_sc[0] <= tgt:
                    emit_scores(next_sc[0])
                    next_sc[0] += 1
                wait("tensor", sc, exp_marks[k])
                eng = acc_eng[kc]
                if kc >= NKC - 2:
                    acc_marks[k] = ("t", None)
                elif kc <= 1:
                    wait("vector", sc, exp_marks[k])
                    if h >= 1:
                        wait("vector", st, summ_marks[h - 1])
                    _dst = F_ACCV if kc == 0 else F_ACCG
                    em("vector", lambda e, _o=PROBS_OFF(k), _d=_dst: e.tensor_copy(
                        F(_d, 1024), A(_o, 1024)).then_inc(sv.h, 1))
                    acc_marks[k] = ("v", sv.inc())
                    if kc == 1:
                        seed_mark = acc_marks[k][1]
                elif eng == "vector":
                    wait("vector", sc, exp_marks[k])
                    if kc % 2 == 0:
                        acc_marks[k] = ("v", acc_marks[k - 1][1])
                    else:
                        wait("vector", sc, exp_marks[k - 1])
                        if kc == 3:
                            wait("vector", sv, seed_mark)
                        par = ((kc - 3) // 2) % 2
                        _pr = B_PRPAIR if par == 0 else B_PRPAIR2
                        _acc = F_ACCV if par == 0 else F_ACCG
                        em("vector", lambda e, _a=PROBS_OFF(k - 1), _b=PROBS_OFF(k),
                           _p=_pr: e.tensor_tensor(
                            A(_p, 1024), A(_a, 1024), A(_b, 1024), OP.add).then_inc(sv.h, 1))
                        _pm = sv.inc()
                        wait("vector", sv, _pm)
                        em("vector", lambda e, _c=_acc, _p=_pr: e.tensor_tensor(
                            F(_c, 1024), F(_c, 1024), A(_p, 1024), OP.add).then_inc(sv.h, 1))
                        acc_marks[k] = ("v", sv.inc())
                        acc_marks[k - 1] = ("v", acc_marks[k][1])
                else:
                    wait("gpsimd", sc, exp_marks[k])
                    if kc == 1 and h >= 1:
                        wait("gpsimd", st, summ_marks[h - 1])
                    if kc == 1:
                        em("gpsimd", lambda e, _o=PROBS_OFF(k): e.tensor_copy(
                            F(F_ACCG, 1024), A(_o, 1024)).then_inc(sg.h, 1))
                    else:
                        em("gpsimd", lambda e, _o=PROBS_OFF(k): e.tensor_tensor(
                            F(F_ACCG, 1024), F(F_ACCG, 1024), A(_o, 1024), OP.add).then_inc(sg.h, 1))
                    acc_marks[k] = ("g", sg.inc())
                for half in range(2):
                    if kc == 0 and h >= 1:
                        wait("tensor", sv, denb_half_marks[h - 1][half])
                    elif kc == 0 and h == 0 and half == 0:
                        wait("tensor", sv, kv_rope_marks[-1])
                    last = half == 1
                    def mkpv(e, _kc=kc, _g=g, _half=half, _last=last, _o=PROBS_OFF(k)):
                        ins = e.matmul(pC[:, _half * 512:(_half + 1) * 512],
                                       A(B_VTOK + _kc * 256 + _g * 128, 128),
                                       A(_o + _half * 512, 512),
                                       start=(_kc == 0), stop=(_kc == NKC - 1))
                        if _last:
                            ins.then_inc(st.h, 1)
                    em("tensor", mkpv)
                pvsum_marks.append(st.inc())
            last_v = max(m for kk, (kind, m) in acc_marks.items()
                         if kind == "v" and h * NKC <= kk < (h + 1) * NKC)
            wait("scalar", st, pvsum_marks[-1])
            if norm_marks:
                wait("scalar", sv, norm_marks[-1])
            pcev_sc_h = []
            for half in range(2):
                em("scalar", lambda e, _h=half: e.activation(
                    PCEV[:, _h * 512:(_h + 1) * 512], pC[:, _h * 512:(_h + 1) * 512],
                    ACT.Copy).then_inc(sc.h, 1))
                pcev_sc_h.append(sc.inc())
            wait("vector", sv, last_v)
            em("vector", lambda e: e.tensor_tensor(F(F_ACCV, 1024), F(F_ACCV, 1024),
                                                   F(F_ACCG, 1024), OP.add).then_inc(sv.h, 1))
            merge_mark = sv.inc()
            wait("tensor", sv, merge_mark)
            if h == 0:
                wait("tensor", sv, ones_bf_mark)
            k30 = h * NKC + NKC - 2
            summ_h = []
            for half in range(2):
                wait("tensor", sc, pcev_sc_h[half])
                em("tensor", lambda e, _half=half, _o=PROBS_OFF(k30): e.matmul(
                    pC[:, _half * 512:(_half + 1) * 512], A(B_ONESBF, 128),
                    A(_o + _half * 512, 512), start=True, stop=False))
                em("tensor", lambda e, _half=half, _o=PROBS_OFF(k30 + 1): e.matmul(
                    pC[:, _half * 512:(_half + 1) * 512], A(B_ONESBF, 128),
                    A(_o + _half * 512, 512), start=False, stop=False))
                def mksum(e, _half=half):
                    e.matmul(pC[:, _half * 512:(_half + 1) * 512], F(F_ONESROW, 128),
                             F(F_ACCV + _half * 512, 512), start=False, stop=True) \
                        .then_inc(st.h, 1)
                em("tensor", mksum)
                summ_h.append(st.inc())
            summ_marks.append(summ_h[1])
            acc_marks[k30] = ("t", summ_marks[-1])
            acc_marks[k30 + 1] = ("t", summ_marks[-1])
            denb_h = []
            for half in range(2):
                wait("vector", st, summ_h[half])
                em("vector", lambda e, _h=half: e.tensor_copy(
                    F(F_RECIPB + _h * 512, 512), pC[:, _h * 512:(_h + 1) * 512])
                   .then_inc(sv.h, 1))
                denb_h.append(sv.inc())
            denb_half_marks.append(denb_h)
            denb_marks.append(denb_h[1])
            rmk = {}
            _dmk = denb_marks[-1]
            _hcap = h

            def _mk_recip(_o, _w, _dmk=_dmk, _rmk=rmk):
                def f():
                    wait("vector", sv, _dmk)
                    em("vector", lambda e: e.reciprocal(F(F_RECIPB + _o, _w),
                                                        F(F_RECIPB + _o, _w))
                       .then_inc(sv.h, 1))
                    _rmk[_o] = sv.inc()
                return f

            def _mk_norm(_o, _hh=_hcap, _rmk=rmk):
                def f():
                    wait("vector", sv, _rmk[_o + 256])
                    em("vector", lambda e: e.tensor_tensor(
                        A(B_ATTNT + _hh * 1024 + _o, 512), PCEV[:, _o:_o + 512],
                        F(F_RECIPB + _o, 512), OP.mult).then_inc(sv.h, 1))
                    norm_marks.append(sv.inc())
                return f

            epi_pend.extend([(5, _mk_recip(0, 256)), (9, _mk_recip(256, 256)),
                             (13, _mk_recip(512, 256)), (17, _mk_recip(768, 256)),
                             (19, _mk_norm(0)), (21, _mk_norm(512))])
        while epi_pend:
            epi_pend.pop(0)[1]()
        v_B_done = sv.n
        if _DBG:
            wait("sync", sv, v_B_done)
            em("sync", lambda e: e.dma_start(
                out=bass.AP(dbg, 2097152, [[8192, 128], [1, 8192]]),
                in_=bass.AP(ab, B_ATTNT, [[NB, 128], [1, 16384]]).bitcast(F32)).then_inc(sdo.h, 16))
            sdo.inc(16)

        wait("sync", sv, v_B_done)
        for t in (6, 7):
            em("sync", lambda e, _t=t: e.dma_start(
                out=F(F_X1 + _t * 2048, 2048),
                in_=bass.AP(x_loc, _t * 128 * D, [[D, 128], [1, D]])).then_inc(sdxl.h, 16))
            xloc_marks[t] = sdxl.inc(16)

        ops_sets = {0: pA, 1: pB}
        wo_mm_marks = []
        osb_marks = []
        add_marks = []
        scan_marks = []
        fw_dma_marks = []
        t2last = {}

        def OSB_AP(i, off=0, w=1024):
            if i % 2 == 0:
                return F(F_OSB + off, w)
            return bass.AP(ab, B_PROBS + 2 * off, [[NB, 128], [1, 2 * w]]).bitcast(F32)

        def c_emit_T(i):
            wait("tensor", sv, osb_marks[i])
            for r in range(2):
                if len(add_marks) >= 2:
                    wait("tensor", sv, add_marks[-2])
                for jt in range(4):
                    tsub = r * 4 + jt
                    last = jt == 3
                    def mkt(e, _i=i, _r=r, _jt=jt, _tsub=tsub, _last=last):
                        ins = e.transpose(
                            pC[:, _r * 512 + _jt * 128:_r * 512 + (_jt + 1) * 128],
                            OSB_AP(_i, _tsub * 128, 128), F(F_IDENTF, 128))
                        if _last:
                            ins.then_inc(st.h, 1)
                    em("tensor", mkt)
                tmark = st.inc()
                wait("vector", st, tmark)
                if i == 0 and r == 0:
                    wait("vector", sdxl, 128)
                def mka(e, _i=i, _r=r):
                    dst = F3(F_X1 + (_r * 4) * 2048 + _i * 128, 2048, 4, 128)
                    e.tensor_tensor(dst, pC[:, _r * 512:_r * 512 + 512], dst, OP.add) \
                        .then_inc(sv.h, 1)
                em("vector", mka)
                add_marks.append(sv.inc())
            t2last[i] = st.n

        for oc in range(NDC):
            if oc >= 2:
                wait("gpsimd", st, wo_mm_marks[oc - 2])
            elif oc == 0:
                wait("gpsimd", st, kvmm_last)
            for half in range(2):
                em("gpsimd", lambda e, _oc=oc, _h=half, _s=sdq[oc % 2].h: e.dma_start(
                    out=A3(B_WCOL + (_oc % 2) * 2048 + _h * 8 * 128, 128, 8, 128),
                    in_=bass.AP(wo, (_h * 8 * 128) * D + _oc * 128,
                                [[D, 128], [128 * D, 8], [1, 128]])).then_inc(_s, 16))
                sdq[oc % 2].inc(16)
            wdma = sdq[oc % 2].n
            if oc >= 2:
                wait("sync", sv, scan_marks[oc - 2])
            em("sync", lambda e, _i=oc, _s=sdf[oc % 2].h: e.dma_start(
                out=bass.AP(ab, B_FWB + (_i % 2) * 4096, [[NB, 128], [1, 4096]]).bitcast(F32),
                in_=bass.AP(fw, _i * 128 * D, [[D, 128], [1, D]])).then_inc(_s, 16))
            fw_dma_marks.append((sdf[oc % 2], sdf[oc % 2].inc(16)))
            P = ops_sets[oc % 2]
            wait("tensor", sdq[oc % 2], wdma)
            if oc == 0:
                wait("tensor", sv, v_B_done)
            if oc >= 2:
                wait("tensor", sv, osb_marks[oc - 2])
            for dc in range(NDC):
                for half in range(2):
                    last = dc == NDC - 1 and half == 1
                    def mko(e, _P=P, _oc=oc, _dc=dc, _h=half, _last=last):
                        ins = e.matmul(_P[:, _h * 512:(_h + 1) * 512],
                                       A(B_WCOL + (_oc % 2) * 2048 + _dc * 128, 128),
                                       A(B_ATTNT + _dc * 1024 + _h * 512, 512),
                                       start=(_dc == 0), stop=(_dc == NDC - 1))
                        if _last:
                            ins.then_inc(st.h, 1)
                    em("tensor", mko)
            wo_mm_marks.append(st.inc())
            wait("vector", st, wo_mm_marks[oc])
            if oc >= 2:
                wait("vector", st, t2last[oc - 2])
            em("vector", lambda e, _P=P, _oc=oc: e.tensor_scalar(
                OSB_AP(_oc), _P[:, 0:1024], F(F_BO + _oc, 1), None, OP.add).then_inc(sv.h, 1))
            osb_marks.append(sv.inc())
            wait("vector", fw_dma_marks[oc][0], fw_dma_marks[oc][1])
            em("vector", lambda e, _i=oc: e.reduce_sum(
                F(F_WABS + _i, 1), bass.AP(ab, B_FWB + (_i % 2) * 4096,
                                           [[NB, 128], [1, 4096]]).bitcast(F32)[:, 0:2048],
                axis=AX, apply_absolute_value=True).then_inc(sv.h, 1))
            scan_marks.append(sv.inc())
            if oc >= 1:
                c_emit_T(oc - 1)
        c_emit_T(NDC - 1)
        v_C_done = sv.n

        if _DBG:
            wait("sync", sv, v_C_done)
            em("sync", lambda e: e.dma_start(
                out=bass.AP(dbg, 4194304, [[16384, 128], [1, 16384]]),
                in_=F(F_X1, 16384)).then_inc(sdo.h, 16))
            sdo.inc(16)

        wait("vector", sv, scan_marks[-1])
        em("vector", lambda e: e.reduce_sum(F(F_WABS1, 1), F(F_WABS, 16), axis=AX).then_inc(sv.h, 1))
        m = sv.inc()
        wait("vector", sv, m)
        em("vector", lambda e: e.tensor_copy(F(F_SPACER, 1), F(F_ONESP, 1)).then_inc(sv.h, 1))
        v_wabs = sv.inc()
        ONESROW_C = lambda: bass.AP(ab, B_VTSCR, [[NB, 1], [1, 256]]).bitcast(F32)[:, 0:128]
        THROW_C = lambda: bass.AP(ab, B_VTSCR + 256, [[NB, 1], [1, 2]]).bitcast(F32)[:, 0:1]
        em("vector", lambda e: e.memset(ONESROW_C(), 1.0).then_inc(sv.h, 1))
        sv.inc()
        wait("tensor", sv, v_wabs)
        wait("tensor", sv, add_marks[-1])
        em("tensor", lambda e: e.matmul(pC[0:1, 0:1], F(F_WABS1, 1), F(F_ONESP, 1),
                                        start=True, stop=True).then_inc(st.h, 1))
        t_tot = st.inc()
        wait("vector", st, t_tot)
        em("vector", lambda e: e.tensor_copy(THROW_C(), pC[0:1, 0:1]).then_inc(sv.h, 1))
        m = sv.inc()
        wait("vector", sv, m)
        em("vector", lambda e: e.tensor_scalar(THROW_C(), THROW_C(),
                                               0.5 / (float(D) * float(D)), 0.5 * QEPS,
                                               OP.mult, OP.add).then_inc(sv.h, 1))
        v_throw = sv.inc()
        wait("tensor", sv, v_throw)
        em("tensor", lambda e: e.matmul(pC[:, 0:1], ONESROW_C(), THROW_C(),
                                        start=True, stop=True).then_inc(st.h, 1))
        t_thr = st.inc()
        wait("vector", st, t_thr)
        em("vector", lambda e: e.tensor_copy(F(F_THR, 1), pC[:, 0:1]))
        em("vector", lambda e: e.tensor_scalar(F(F_NTHR, 1), pC[:, 0:1], -1.0, None, OP.mult)
           .then_inc(sv.h, 1))
        v_thr = sv.inc()

        xn_tp_marks = []
        wait("scalar", st, t2last[NDC - 1])
        YB = lambda par: bass.AP(ab, B_Y + par * 4096, [[NB, 128], [1, 4096]]).bitcast(F32)
        d1s = {}
        d1f = {}
        d2s = {}
        d2a = {}

        def d2_finish_T(t):
            am = p_finish(2 + t % 2, (lambda _t=t: (lambda: YB(_t % 2)))(),
                          (lambda: A(B_XLN, 2048)), d2s[t][0], d2s[t][1],
                          apply_gate_st=last_tp_st2[0])
            d2a[t] = am
            dest_fn = (lambda _t=t: (lambda r: A3(B_XN2T + (4 * r) * 1024 + _t * 128,
                                                  1024, 4, 128)))()
            p_transpose16(B_XLN, dest_fn, am)
            last_tp_st2[0] = st.n

        last_tp_st2 = [last_tp_st]
        for t in range(NT_L):
            src_fn = (lambda _t=t: (lambda: F(F_X1 + _t * 2048, 2048)))()
            d1s[t] = p_stats(t % 2, src_fn, (sv, v_C_done))
            if t >= 1:
                tm = t - 1
                yre = d2a.get(tm - 2, 0)
                d1f[tm] = p_finish(tm % 2, (lambda _t=tm: (lambda: F(F_X1 + _t * 2048, 2048)))(),
                                   (lambda _t=tm: (lambda: YB(_t % 2)))(), d1s[tm][0], d1s[tm][1],
                                   apply_gate_st=0, apply_gate_sv=yre)
                d2s[tm] = p_stats(2 + tm % 2, (lambda _t=tm: (lambda: YB(_t % 2)))(),
                                  (sv, d1f[tm]))
            if t >= 2:
                d2_finish_T(t - 2)
        tm = NT_L - 1
        d1f[tm] = p_finish(tm % 2, (lambda _t=tm: (lambda: F(F_X1 + _t * 2048, 2048)))(),
                           (lambda _t=tm: (lambda: YB(_t % 2)))(), d1s[tm][0], d1s[tm][1],
                           apply_gate_st=0, apply_gate_sv=d2a.get(tm - 2, 0))
        d2s[tm] = p_stats(2 + tm % 2, (lambda _t=tm: (lambda: YB(_t % 2)))(), (sv, d1f[tm]))
        d2_finish_T(NT_L - 2)
        d2_finish_T(NT_L - 1)
        geng, gmv = tp_gate2[-1]
        wait("vector", sv if geng == "v" else sc, gmv)
        geng, gmv = tp_gate2[-2]
        wait("vector", sv if geng == "v" else sc, gmv)
        for dc in range(NDC):
            em("vector", lambda e, _dc=dc: e.tensor_scalar(
                A(B_XN2T + _dc * 1024, 1024), A(B_XN2T + _dc * 1024, 1024), F(F_GAM + _dc, 1),
                None, OP.mult).then_inc(sv.h, 1))
            sv.inc()
        v_D_done = sv.n
        if _DBG:
            wait("sync", sv, v_D_done)
            em("sync", lambda e: e.dma_start(
                out=bass.AP(dbg, 3145728, [[8192, 128], [1, 8192]]),
                in_=bass.AP(ab, B_XN2T, [[NB, 128], [1, 16384]]).bitcast(F32)).then_inc(sdo.h, 16))
            sdo.inc(16)

        quant_marks = []
        wqt_evac_marks = []
        cb_marks = []
        ffn_marks = []
        fsb_marks = []
        eadd_marks = []
        e_t2last = {}

        def e_emit_T2(i):
            wait("tensor", sv, fsb_marks[i])
            for r in range(2):
                if len(eadd_marks) >= 2:
                    wait("tensor", sv, eadd_marks[-2])
                elif i == 0 and r == 0:
                    wait("tensor", sv, add_marks[-1])
                    wait("tensor", sv, v_thr)
                for jt in range(4):
                    tsub = r * 4 + jt
                    last = jt == 3
                    def mkt2(e, _i=i, _r=r, _jt=jt, _tsub=tsub, _last=last):
                        ins = e.transpose(
                            pC[:, _r * 512 + _jt * 128:_r * 512 + (_jt + 1) * 128],
                            OSB_AP(_i, _tsub * 128, 128), F(F_IDENTF, 128))
                        if _last:
                            ins.then_inc(st.h, 1)
                    em("tensor", mkt2)
                tmark = st.inc()
                wait("vector", st, tmark)
                def mka2(e, _i=i, _r=r):
                    dst = F3(F_X1 + (_r * 4) * 2048 + _i * 128, 2048, 4, 128)
                    e.tensor_tensor(dst, pC[:, _r * 512:_r * 512 + 512], dst, OP.add) \
                        .then_inc(sv.h, 1)
                em("vector", mka2)
                eadd_marks.append(sv.inc())
            e_t2last[i] = st.n

        EFW = lambda _i: (B_FWB + (_i % 4) * 4096) if (_i % 4) < 2 else (B_WK + ((_i % 4) - 2) * 4096)
        Q1 = lambda: bass.AP(ab, B_QSCR, [[NB, 128], [1, 4096]]).bitcast(F32)
        Q2 = lambda: bass.AP(ab, B_QSCR + 4096, [[NB, 128], [1, 4096]]).bitcast(F32)
        e_dma_marks = []

        def e_dma(i):
            if i >= NDC:
                return
            if i >= 4:
                wait("sync", sv, quant_marks[i - 4])
            elif i < 2:
                wait("sync", sv, scan_marks[-1])
            em("sync", lambda e, _i=i, _s=sdf[i % 4].h: e.dma_start(
                out=bass.AP(ab, EFW(_i), [[NB, 128], [1, 4096]]).bitcast(F32),
                in_=bass.AP(fw, _i * 128 * D, [[D, 128], [1, D]])).then_inc(_s, 16))
            e_dma_marks.append(sdf[i % 4].inc(16))

        def e_quant(i):
            if i >= NDC:
                return
            wait("vector", sdf[i % 4], e_dma_marks[i])
            if i == 0:
                wait("vector", sv, v_thr)
            FWAP = (lambda _i=i: (lambda: bass.AP(ab, EFW(_i),
                                                  [[NB, 128], [1, 4096]]).bitcast(F32)))()
            em("vector", lambda e, f=FWAP: e.tensor_scalar(Q1(), f(), F(F_THR, 1), None, OP.is_gt))
            em("vector", lambda e, f=FWAP: e.tensor_scalar(Q2(), f(), F(F_NTHR, 1), None, OP.is_lt)
               .then_inc(sv.h, 1))
            _mq2 = sv.inc()
            wait("vector", sv, _mq2)
            em("vector", lambda e, _i=i: e.tensor_sub(A(B_WQROW + (_i % 2) * 2048, 2048),
                                                      Q1(), Q2()).then_inc(sv.h, 1))
            quant_marks.append(sv.inc())

        e_dma(0)
        e_quant(0)
        e_dma(1)
        for oc in range(NDC):
            wait("tensor", sv, quant_marks[oc])
            for r in range(4):
                half = r % 2
                if len(wqt_evac_marks) >= 2:
                    wait("tensor", sv, wqt_evac_marks[-2])
                elif r == 0 and oc == 0:
                    _ge, _gm = tp_gate2[-1]
                    wait("tensor", sv if _ge == "v" else sc, _gm)
                    _ge, _gm = tp_gate2[-2]
                    wait("tensor", sv if _ge == "v" else sc, _gm)
                for cth in range(4):
                    ic = 4 * r + cth
                    last = cth == 3
                    def mkq(e, _oc=oc, _ic=ic, _half=half, _cth=cth, _last=last):
                        dstv = pA[:, _half * 512:(_half + 1) * 512].bitcast(BF16)
                        ins = e.transpose(dstv[:, _cth * 128:(_cth + 1) * 128],
                                          A(B_WQROW + (_oc % 2) * 2048 + _ic * 128, 128),
                                          A(B_IDENT, 128))
                        if _last:
                            ins.then_inc(st.h, 1)
                    em("tensor", mkq)
                tmark = st.inc()
                wait("vector", st, tmark)
                em("vector", lambda e, _oc=oc, _r=r, _half=half: e.tensor_copy(
                    A(B_WQT + (_oc % 2) * 2048 + _r * 512, 512),
                    pA[:, _half * 512:(_half + 1) * 512].bitcast(BF16)[:, 0:512]).then_inc(sv.h, 1))
                wqt_evac_marks.append(sv.inc())
            e_quant(oc + 1)
            e_dma(oc + 2)
            wait("tensor", sv, wqt_evac_marks[-1])
            if oc >= 1:
                wait("tensor", sv, cb_marks[oc - 1])
            for ic in range(NDC):
                last = ic == NDC - 1
                def mkc(e, _oc=oc, _ic=ic, _last=last):
                    ins = e.matmul(pD[:, 0:1], A(B_WQT + (_oc % 2) * 2048 + _ic * 128, 128),
                                   A(B_BETBF + _ic, 1), start=(_ic == 0), stop=(_ic == NDC - 1))
                    if _last:
                        ins.then_inc(st.h, 1)
                em("tensor", mkc)
            cmark = st.inc()
            wait("vector", st, cmark)
            em("vector", lambda e, _oc=oc: e.tensor_scalar(F(F_CB + (_oc % 2), 1), pD[:, 0:1],
                                                           F(F_FB + _oc, 1), None, OP.add)
               .then_inc(sv.h, 1))
            cb_marks.append(sv.inc())
            if oc >= 1:
                wait("tensor", sv, fsb_marks[-1])
            else:
                wait("tensor", sv, osb_marks[-1])
                wait("tensor", sv, v_D_done)
            for ic in range(NDC):
                for half in range(2):
                    last = ic == NDC - 1 and half == 1
                    def mkf(e, _oc=oc, _ic=ic, _h=half, _last=last):
                        ins = e.matmul(pB[:, _h * 512:(_h + 1) * 512],
                                       A(B_WQT + (_oc % 2) * 2048 + _ic * 128, 128),
                                       A(B_XN2T + _ic * 1024 + _h * 512, 512),
                                       start=(_ic == 0), stop=(_ic == NDC - 1))
                        if _last:
                            ins.then_inc(st.h, 1)
                    em("tensor", mkf)
            ffn_marks.append(st.inc())
            wait("vector", st, ffn_marks[oc])
            wait("vector", sv, cb_marks[oc])
            if oc >= 2:
                wait("vector", st, e_t2last[oc - 2])
            em("vector", lambda e, _oc=oc: e.tensor_scalar(
                OSB_AP(_oc), pB[:, 0:1024], F(F_CB + (_oc % 2), 1), None, OP.add).then_inc(sv.h, 1))
            fsb_marks.append(sv.inc())
            if oc >= 1:
                e_emit_T2(oc - 1)
        e_emit_T2(NDC - 1)
        v_E_done = sv.n

        wait("sync", sv, v_E_done)
        for t in range(NT_L):
            em("sync", lambda e, _t=t: e.dma_start(
                out=bass.AP(out, _t * 128 * D, [[D, 128], [1, D]]),
                in_=F(F_X1 + _t * 2048, 2048)).then_inc(sdo.h, 16))
            sdo.inc(16)
        em("sync", lambda e, _v=sdo.n: e.wait_ge(sdo.h, _v))

        if _OPS_HOOK is not None:
            _OPS_HOOK(ops, {"sdx": sdx, "sdw": sdw, "sg": sg, "sv": sv, "sc": sc, "st": st})
            return nc

        @block.sync
        def _(e):
            for fn in ops["sync"]:
                fn(e)

        @block.gpsimd
        def _(e):
            for fn in ops["gpsimd"]:
                fn(e)

        @block.vector
        def _(e):
            for fn in ops["vector"]:
                fn(e)

        @block.scalar
        def _(e):
            for fn in ops["scalar"]:
                fn(e)

        @block.tensor
        def _(e):
            for fn in ops["tensor"]:
                fn(e)

    return nc


_OPS_HOOK = None
_DBG = False
_LAST_RES = None
_NC_CACHE = None


def kernel(L0_attn_x, L0_attn_wq, L0_attn_bq, L0_attn_wk, L0_attn_bk, L0_attn_wv,
           L0_attn_bv, L0_attn_wo, L0_attn_bo, L0_ffn_weight, L0_ffn_bias,
           L0_ffn_ln_gamma, L0_ffn_ln_beta):
    global _NC_CACHE
    x = np.ascontiguousarray(np.asarray(L0_attn_x, dtype=np.float32))
    shared = {
        "wq": np.ascontiguousarray(np.asarray(L0_attn_wq, np.float32)),
        "wk": np.ascontiguousarray(np.asarray(L0_attn_wk, np.float32)),
        "wv": np.ascontiguousarray(np.asarray(L0_attn_wv, np.float32)),
        "wo": np.ascontiguousarray(np.asarray(L0_attn_wo, np.float32)),
        "bq": np.ascontiguousarray(np.asarray(L0_attn_bq, np.float32)),
        "bk": np.ascontiguousarray(np.asarray(L0_attn_bk, np.float32)),
        "bv": np.ascontiguousarray(np.asarray(L0_attn_bv, np.float32)),
        "bo": np.ascontiguousarray(np.asarray(L0_attn_bo, np.float32)),
        "fw": np.ascontiguousarray(np.asarray(L0_ffn_weight, np.float32)),
        "fb": np.ascontiguousarray(np.asarray(L0_ffn_bias, np.float32)),
        "gam": np.ascontiguousarray(np.asarray(L0_ffn_ln_gamma, np.float32)),
        "bet": np.ascontiguousarray(np.asarray(L0_ffn_ln_beta, np.float32)),
    }
    if _NC_CACHE is None:
        _NC_CACHE = build_nc()
    nc = _NC_CACHE
    in_maps = []
    for c in range(8):
        b = c // 4
        q0 = SL * (c % 4)
        im = dict(shared)
        im["x_full"] = np.ascontiguousarray(x[b])
        im["x_loc"] = np.ascontiguousarray(x[b, q0:q0 + SL])
        im["pos_base"] = np.array([[float(q0)]], dtype=np.float32)
        in_maps.append(im)
    res = run_bass_kernel_spmd(nc, in_maps, core_ids=list(range(8)))
    global _LAST_RES
    _LAST_RES = res
    outa = np.empty((2, S, D), np.float32)
    for c in range(8):
        b = c // 4
        q0 = SL * (c % 4)
        outa[b, q0:q0 + SL] = res.results[c]["out"]
    return outa


if __name__ == "__main__":
    nc = build_nc()
    print("built ok; sbuf remaining:", nc.sbuf_bytes_remaining)
